# version 04 (final)
# baseline (speedup 1.0000x reference)
from contextlib import ExitStack

import numpy as np
import concourse.bass as bass
import concourse.mybir as mybir
from concourse.bass_utils import run_bass_kernel_spmd

F32 = mybir.dt.float32
BF16 = mybir.dt.bfloat16
AF = mybir.ActivationFunctionType
ALU = mybir.AluOpType
AXX = mybir.AxisListType.X

NCORES = 8
NL, D, DI, N, R, KC = 4, 1024, 2048, 16, 64, 4
B, T = 2, 8192
TC = T // 4
FF = 2816
DDC = DI // NCORES
LNE_OUT = 1e-5
LNE_B = 1e-3
RMS_EPS = 1e-6
HALO = 4

ENGINES = ("sync", "scalar", "vector", "tensor", "gpsimd")
DMA_QUEUES = ("sync", "scalar", "gpsimd")


class Prog:

    def __init__(self, nc):
        self.nc = nc
        self.ops = []
        self.count = {e: 0 for e in ENGINES}
        self.dma_count = {}
        self.last_w = {}
        self.readers = {}
        self.sems = {}
        self.exclusive = set()

    def mark_exclusive(self, *names):
        self.exclusive.update(names)

    def _split_excl(self, reads, writes):
        r = [x for x in reads if x not in self.exclusive]
        w = list(writes) + [x for x in reads if x in self.exclusive]
        return r, w

    def _deps(self, reads, writes, extra):
        deps = list(extra or [])
        for r in reads:
            t = self.last_w.get(r)
            if t is not None:
                deps.append(t)
        for w in writes:
            t = self.last_w.get(w)
            if t is not None:
                deps.append(t)
            deps.extend(self.readers.get(w, ()))
        return deps

    def _book(self, tok, reads, writes):
        for r in reads:
            self.readers.setdefault(r, []).append(tok)
        for w in writes:
            self.last_w[w] = tok
            self.readers[w] = []

    def on(self, eng, fn, reads=(), writes=(), deps=(), inc=1):
        reads, writes = self._split_excl(reads, writes)
        d = self._deps(reads, writes, deps)
        self.ops.append((eng, d, fn, inc, False))
        self.count[eng] += inc
        tok = ("c", eng, self.count[eng])
        self._book(tok, reads, writes)
        return tok

    def dma(self, eng, fn, reads=(), writes=(), deps=(), channel=None):
        if channel is None:
            channel = writes[0] if writes else ("st_" + (reads[0] if reads else eng))
        channel = "ch_" + channel
        reads, writes = self._split_excl(reads, writes)
        d = self._deps(reads, writes, deps)
        self.dma_count[channel] = self.dma_count.get(channel, 0) + 16
        self.ops.append((eng, d, fn, 16, channel))
        tok = ("d", channel, self.dma_count[channel])
        self._book(tok, reads, writes)
        return tok

    def finalize(self, stack, block):
        nc = self.nc
        for e in ENGINES:
            self.sems[("c", e)] = stack.enter_context(nc.semaphore(f"S{e}"))
        for ch in self.dma_count:
            self.sems[("d", ch)] = stack.enter_context(nc.semaphore(ch))
        per_eng = {e: [] for e in ENGINES}
        for op in self.ops:
            per_eng[op[0]].append(op)

        def mk(eng_name, ops):
            def run(h):
                waited = {}
                for (_e, deps, fn, inc, is_dma) in ops:
                    need = {}
                    for (kind, de, cnt) in deps:
                        k = (kind, de)
                        if cnt > need.get(k, -1):
                            need[k] = cnt
                    for k, cnt in need.items():
                        if waited.get(k, -1) < cnt:
                            h.wait_ge(self.sems[k], cnt)
                            waited[k] = cnt
                    if is_dma:
                        fn(h).then_inc(self.sems[("d", is_dma)], 16)
                    else:
                        fn(h).then_inc(self.sems[("c", eng_name)], inc)
            return run

        finals = dict(self.dma_count)

        def mk_final(eng_name, ops):
            base = mk(eng_name, ops)

            def run(h):
                base(h)
                for ch, cnt in finals.items():
                    h.wait_ge(self.sems[("d", ch)], cnt)
            return run

        for e in ENGINES:
            if e == "sync":
                getattr(block, e)(mk_final(e, per_eng[e]))
            elif per_eng[e]:
                getattr(block, e)(mk(e, per_eng[e]))


class Phase:
    _cache = {}

    def __init__(self, name, build_fn, **kw):
        self.key = (name,) + tuple(sorted(kw.items()))
        self.build = build_fn
        self.kw = kw

    def nc(self):
        if self.key not in Phase._cache:
            nc = bass.Bass("TRN2", target_bir_lowering=False, debug=False)
            stack = ExitStack()
            with stack:
                block = stack.enter_context(nc.Block())
                p = Prog(nc)
                self.build(p, nc, stack, **self.kw)
                p.finalize(stack, block)
            mybir.codegen_inst_isa_subclasses(nc)
            Phase._cache[self.key] = nc
        return Phase._cache[self.key]

    def run(self, in_maps):
        res = run_bass_kernel_spmd(self.nc(), in_maps, core_ids=list(range(len(in_maps))))
        return res.results


def chunks(total, step):
    out = []
    o = 0
    while o < total:
        out.append((o, min(step, total - o)))
        o += step
    return out


class Consts:

    def __init__(self, p, nc, stack, maxn=8):
        self.p = p
        self.buf = stack.enter_context(nc.sbuf_tensor("CONSTS", [128, maxn], F32))
        self.vals = {}

    def col(self, v):
        v = float(v)
        if v not in self.vals:
            i = len(self.vals)
            self.vals[v] = i
            self.p.on("vector", lambda vec, i=i, v=v: vec.memset(self.buf[:, i:i + 1], v),
                      writes=["const"])
        return self.buf[:, self.vals[v]:self.vals[v] + 1]


def pe_bcast(p, row_ap, row_res, out_ap, out_res, psum, psum_res, ones_row, F,
             evac="vector", mm_chunk=512):
    for (o, n) in chunks(F, mm_chunk):
        p.on("tensor", lambda t, o=o, n=n: t.matmul(
            psum[:, 0:n], ones_row, row_ap[0:1, o:o + n], start=True, stop=True),
            reads=[row_res, "ONESROW"], writes=[psum_res])
        if evac == "vector":
            p.on("vector", lambda v, o=o, n=n: v.tensor_copy(out_ap[:, o:o + n], psum[:, 0:n]),
                 reads=[psum_res], writes=[out_res])
        else:
            p.on("scalar", lambda sc, o=o, n=n: sc.activation(out_ap[:, o:o + n], psum[:, 0:n], AF.Copy),
                 reads=[psum_res], writes=[out_res])


def rsqrt_row(p, cst, row_io, res, eps):
    np_ = row_io.shape[0]
    ce = cst.col(eps)[0:np_]
    ch = cst.col(-0.5)[0:np_]
    p.on("scalar", lambda s: s.activation(row_io, row_io, AF.Ln, bias=ce),
         reads=[res, "const"], writes=[res])
    p.on("scalar", lambda s: s.activation(row_io, row_io, AF.Exp, scale=ch),
         reads=[res, "const"], writes=[res])



def _prep_weights(inp):
    c = np.ascontiguousarray
    w = {}
    w["in_wT"] = c(np.transpose(inp["L0_attn_a_in_w"], (0, 2, 1)))
    w["xprojT"] = c(np.transpose(inp["L0_attn_a_xproj_w"], (0, 2, 1)))
    w["dt_wT"] = c(np.transpose(inp["L0_attn_a_dt_w"], (0, 2, 1)))
    w["out_wT"] = c(np.transpose(inp["L0_attn_a_out_w"], (0, 2, 1)))
    w["conv_w"] = c(inp["L0_attn_a_conv_w"][:, :, 0, :])
    w["conv_b"] = c(inp["L0_attn_a_conv_b"].reshape(NL, DI, 1))
    w["norm_w"] = c(inp["L0_attn_a_norm_w"].reshape(NL, D, 1))
    w["A_log"] = c(inp["L0_attn_a_A_log"])
    w["Dp"] = c(inp["L0_attn_a_Dp"].reshape(NL, DI, 1))
    w["dt_b"] = c(inp["L0_attn_a_dt_b"].reshape(NL, DI, 1))
    w["final_w"] = c(inp["L0_attn_a_final_w"].reshape(D, 1))
    w["w_gate"] = c(inp["L0_ffn_w_gate"])
    w["w_up"] = c(inp["L0_ffn_w_up"])
    w["w_down"] = c(inp["L0_ffn_w_down"])
    for k in ["ln1_g", "ln1_b", "ln2_g", "ln2_b", "tm_mix_k", "tm_mix_v",
              "tm_mix_r", "time_decay", "time_first", "cm_mix_k", "cm_mix_r",
              "tm_Wk", "tm_Wv", "tm_Wr", "Wo", "cm_Wk", "cm_Wv", "cm_Wr"]:
        w[k] = c(inp["L0_attn_b_" + k])
    return w


def shard_halo(xT, halo=HALO):
    out = []
    for k in range(NCORES):
        b, j = divmod(k, 4)
        t0 = j * TC
        h = (np.zeros((xT.shape[1], halo), xT.dtype) if j == 0
             else xT[b, :, t0 - halo:t0])
        out.append(np.ascontiguousarray(np.concatenate([h, xT[b, :, t0:t0 + TC]], 1)))
    return out


def unshard(parts, Bq, Dq, Tq):
    out = np.zeros((Bq, Dq, Tq), parts[0].dtype)
    for k in range(NCORES):
        b, j = divmod(k, 4)
        out[b, :, j * TC:(j + 1) * TC] = parts[k]
    return out



def build_apre(p, nc, stack, outer_ln=False, stop=99):
    TCH = HALO + TC
    HTC = TC // 2
    HH2 = HALO + HTC
    KD, KI = D // 128, DI // 128
    NE = 2 * DI // 128

    xh = nc.dram_tensor("xh", [D, TCH], F32, kind="ExternalInput").ap()
    in_wT = nc.dram_tensor("in_wT", [D, 2 * DI], F32, kind="ExternalInput").ap()
    xprojT = nc.dram_tensor("xprojT", [DI, R + 2 * N], F32, kind="ExternalInput").ap()
    dt_wT = nc.dram_tensor("dt_wT", [R, DI], F32, kind="ExternalInput").ap()
    conv_w = nc.dram_tensor("conv_w", [DI, KC], F32, kind="ExternalInput").ap()
    conv_b = nc.dram_tensor("conv_b", [DI, 1], F32, kind="ExternalInput").ap()
    dt_b = nc.dram_tensor("dt_b", [DI, 1], F32, kind="ExternalInput").ap()
    norm_g = nc.dram_tensor("norm_g", [D, 1], F32, kind="ExternalInput").ap()

    O_delta = nc.dram_tensor("delta", [DI, TC], BF16, kind="ExternalOutput").ap()
    O_dxc = nc.dram_tensor("dxc", [DI, TC], BF16, kind="ExternalOutput").ap()
    O_bc = nc.dram_tensor("bc", [2 * N, TC], F32, kind="ExternalOutput").ap()
    O_xcs = nc.dram_tensor("xcs", [DI, TC], BF16, kind="ExternalOutput").ap()
    O_zs = nc.dram_tensor("zs", [DI, TC], BF16, kind="ExternalOutput").ap()
    O_xa = (nc.dram_tensor("xa", [D, TC], F32, kind="ExternalOutput").ap()
            if outer_ln else None)

    sb = lambda n, s_, d=F32: stack.enter_context(nc.sbuf_tensor(n, s_, d))
    X = sb("X", [128, KD, HH2])
    XN = sb("XN", [128, KD, HH2], BF16)
    WS = sb("WS", [128, 2, KD, 128], BF16)
    G = sb("G", [128, KD])
    XCP = sb("XCP", [128, 2, HH2])
    ZSB = sb("ZSB", [128, 2, HTC], BF16)
    XCS = sb("XCS", [128, KI, HTC], BF16)
    CACC = sb("CACC", [128, HTC])
    CW = sb("CW", [128, KI, KC])
    CB = sb("CB", [128, KI])
    DTB = sb("DTB", [128, KI])
    XPD = sb("XPD", [64, HTC], BF16)
    XBC = sb("XBC", [32, HTC])
    XPJ = sb("XPJ", [128, KI, R + 2 * N], BF16)
    DTW = sb("DTW", [R, DI], BF16)
    SPT = sb("SPT", [128, 512])
    SQ1 = sb("SQ1", [128, 512])
    DLT = sb("DLT", [128, 2, HTC], BF16)
    DXC = sb("DXC", [128, 2, HTC], BF16)
    ROWS = sb("ROWS", [1, 2 * HH2])
    BRD = sb("BRD", [128, 2 * HH2])
    ONE = sb("ONE", [128, 1])
    ONR = sb("ONR", [1, 128])

    ps = [stack.enter_context(nc.psum_tensor(f"ps{i}", [128, 512], F32)) for i in range(4)]
    psS = stack.enter_context(nc.psum_tensor("psS", [128, 512], F32))

    cst = Consts(p, nc, stack)
    p.mark_exclusive("b0", "b1", "b2", "b3", "psS")
    p.dma("sync", lambda s: s.dma_start(
        out=G[:], in_=norm_g.rearrange("(k p) o -> p (k o)", p=128), allow_slow_non_contiguous=True), writes=["G"])
    p.dma("sync", lambda s: s.dma_start(
        out=CW[:], in_=conv_w.rearrange("(k p) c -> p k c", p=128)), writes=["CW"])
    p.dma("sync", lambda s: s.dma_start(
        out=CB[:], in_=conv_b.rearrange("(k p) o -> p (k o)", p=128), allow_slow_non_contiguous=True), writes=["CB"])
    p.dma("sync", lambda s: s.dma_start(
        out=DTB[:], in_=dt_b.rearrange("(k p) o -> p (k o)", p=128), allow_slow_non_contiguous=True), writes=["DTB"])
    p.dma("gpsimd", lambda g: g.dma_start(
        out=XPJ[:], in_=xprojT.rearrange("(k p) e -> p k e", p=128)),
        writes=["XPJ"])
    p.dma("gpsimd", lambda g: g.dma_start(out=DTW[:], in_=dt_wT),
        writes=["DTW"])
    p.on("vector", lambda v: v.memset(ONE[:], 1.0), writes=["ONE"])
    p.on("vector", lambda v: v.memset(ONR[:], 1.0), writes=["ONESROW"])

    if stop < 2:
        p.dma("sync", lambda s: s.dma_start(
            out=X[:], in_=xh[:, 0:HH2].rearrange("(k p) t -> p k t", p=128)), writes=["X"])
        p.dma("sync", lambda s: s.dma_start(
            out=O_xa[:, 0:HH2].rearrange("(k p) t -> p k t", p=128), in_=X[:]), reads=["X"])
        return
    sx = ROWS[:, 0:HH2]
    r0 = ROWS[:, HH2:2 * HH2]
    brd0 = BRD[:, 0:HH2]
    brd1 = BRD[:, HH2:2 * HH2]

    def colsums(square, out_row, out_res):
        for (o, n) in chunks(HH2, 512):
            if square:
                for k in range(KD):
                    p.on("scalar", lambda s, k=k, o=o, n=n: s.activation(
                        SQ1[:, 0:n], X[:, k, o:o + n], AF.Square),
                        reads=["X"], writes=["SQ1"])
                    p.on("tensor", lambda t, k=k, o=o, n=n: t.matmul(
                        psS[0:1, 0:n], ONE[:], SQ1[:, 0:n],
                        start=(k == 0), stop=(k == KD - 1)),
                        reads=["SQ1", "ONE"], writes=["psS"])
            else:
                p.on("tensor", lambda t, o=o, n=n: [
                    t.matmul(psS[0:1, 0:n], ONE[:], X[:, k, o:o + n],
                             start=(k == 0), stop=(k == KD - 1)) for k in range(KD)][-1],
                    reads=["X", "ONE"], writes=["psS"])
            p.on("vector", lambda v, o=o, n=n: v.tensor_copy(
                out_row[0:1, o:o + n], psS[0:1, 0:n]), reads=["psS"], writes=[out_res])

    for half in range(2):
        hoff = half * HTC
        p.dma("sync", lambda s, hoff=hoff: s.dma_start(
            out=X[:], in_=xh[:, hoff:hoff + HH2].rearrange("(k p) t -> p k t", p=128)),
            writes=["X"])
        if outer_ln:
            colsums(False, sx, "sx")
            colsums(True, r0, "r0")
            p.on("vector", lambda v: v.tensor_scalar(sx, sx, 1.0 / D, None, ALU.mult),
                 reads=["sx"], writes=["sx"])
            p.on("vector", lambda v: v.tensor_scalar(r0, r0, 1.0 / D, None, ALU.mult),
                 reads=["r0"], writes=["r0"])
            p.on("vector", lambda v: v.tensor_tensor(BRD[0:1, 0:HH2], sx, sx, ALU.mult),
                 reads=["sx"], writes=["BRDtmp"])
            p.on("vector", lambda v: v.tensor_tensor(r0, r0, BRD[0:1, 0:HH2], ALU.subtract),
                 reads=["r0", "BRDtmp"], writes=["r0"])
            rsqrt_row(p, cst, r0, "r0", LNE_OUT)
            p.on("vector", lambda v: v.tensor_tensor(sx, sx, r0, ALU.mult),
                 reads=["sx", "r0"], writes=["sx"])
            pe_bcast(p, r0, "r0", brd0, "BRD0", psS, "psS", ONR[:], HH2)
            pe_bcast(p, sx, "sx", brd1, "BRD1", psS, "psS", ONR[:], HH2)
            for k in range(KD):
                p.on("vector", lambda v, k=k: v.tensor_tensor(
                    X[:, k, :], X[:, k, :], brd0, ALU.mult), reads=["X", "BRD0"], writes=["X"])
                p.on("vector", lambda v, k=k: v.tensor_tensor(
                    X[:, k, :], X[:, k, :], brd1, ALU.subtract), reads=["X", "BRD1"], writes=["X"])
            p.dma("sync", lambda s, hoff=hoff: s.dma_start(
                out=O_xa[:, hoff:hoff + HTC].rearrange("(k p) t -> p k t", p=128),
                in_=X[:, :, HALO:]), reads=["X"])

        if stop < 3:
            continue
        colsums(True, r0, "r0")
        p.on("vector", lambda v: v.tensor_scalar(r0, r0, 1.0 / D, None, ALU.mult),
             reads=["r0"], writes=["r0"])
        rsqrt_row(p, cst, r0, "r0", RMS_EPS)
        pe_bcast(p, r0, "r0", brd0, "BRD0", psS, "psS", ONR[:], HH2)
        for k in range(KD):
            p.on("vector", lambda v, k=k: v.scalar_tensor_tensor(
                XN[:, k, :], X[:, k, :], G[:, k:k + 1], brd0, ALU.mult, ALU.mult),
                reads=["X", "G", "BRD0"], writes=["XN"])

        CH_H = chunks(HH2, 512)
        CH_T = chunks(HTC, 512)
        if stop < 4:
            continue
        for e in range(NE):
            buf = e % 2
            wres = f"WS{buf}"
            p.dma("gpsimd", lambda g, e=e, buf=buf: g.dma_start(
                out=WS[:, buf], in_=in_wT[:, e * 128:(e + 1) * 128].rearrange("(k p) f -> p k f", p=128)),
                writes=[wres])
            is_xc = e < KI
            if is_xc:
                for ci, (o, n) in enumerate(CH_H):
                    bank, bres = ps[ci % 2], f"b{ci % 2}"
                    p.on("tensor", lambda t, o=o, n=n, bank=bank, buf=buf: [
                        t.matmul(bank[:, 0:n], WS[:, buf, k, :], XN[:, k, o:o + n],
                                 start=(k == 0), stop=(k == KD - 1)) for k in range(KD)][-1],
                        reads=[wres, "XN"], writes=[bres])
                    p.on("vector", lambda v, o=o, n=n, bank=bank, buf=buf: v.tensor_copy(
                        XCP[:, buf, o:o + n], bank[:, 0:n]),
                        reads=[bres], writes=[f"XCP{buf}"])
                p.on("vector", lambda v, e=e, buf=buf: v.tensor_scalar(
                    CACC[:], XCP[:, buf, HALO - 3:HALO - 3 + HTC], CW[:, e, 0:1], None, ALU.mult),
                    reads=[f"XCP{buf}", "CW"], writes=["CACC"])
                for j in range(1, KC):
                    p.on("vector", lambda v, e=e, j=j, buf=buf: v.scalar_tensor_tensor(
                        CACC[:], XCP[:, buf, HALO - 3 + j:HALO - 3 + j + HTC], CW[:, e, j:j + 1],
                        CACC[:], ALU.mult, ALU.add),
                        reads=[f"XCP{buf}", "CW", "CACC"], writes=["CACC"])
                p.on("scalar", lambda s, e=e: s.activation(
                    XCS[:, e, :], CACC[:], AF.Silu, bias=CB[:, e:e + 1]),
                    reads=["CACC", "CB"], writes=["XCS"])
                p.dma("gpsimd", lambda g, e=e, hoff=hoff: g.dma_start(
                    out=O_xcs[e * 128:(e + 1) * 128, hoff:hoff + HTC], in_=XCS[:, e, :]),
                    reads=["XCS"])
            else:
                for ci, (o, n) in enumerate(CH_T):
                    bank, bres = ps[2 + ci % 2], f"b{2 + ci % 2}"
                    p.on("tensor", lambda t, o=o, n=n, bank=bank, buf=buf: [
                        t.matmul(bank[:, 0:n], WS[:, buf, k, :], XN[:, k, HALO + o:HALO + o + n],
                                 start=(k == 0), stop=(k == KD - 1)) for k in range(KD)][-1],
                        reads=[wres, "XN"], writes=[bres])
                    p.on("scalar", lambda s, o=o, n=n, bank=bank, buf=buf: s.activation(
                        ZSB[:, buf, o:o + n], bank[:, 0:n], AF.Silu),
                        reads=[bres], writes=[f"ZSB{buf}"])
                p.dma("gpsimd", lambda g, e=e, buf=buf, hoff=hoff: g.dma_start(
                    out=O_zs[(e - KI) * 128:(e - KI + 1) * 128, hoff:hoff + HTC],
                    in_=ZSB[:, buf, :]), reads=[f"ZSB{buf}"])

        if stop < 5:
            continue
        for ci, (o, n) in enumerate(CH_T):
            bank, bres = ps[ci % 2], f"b{ci % 2}"
            p.on("tensor", lambda t, o=o, n=n, bank=bank: [
                t.matmul(bank[0:R + 2 * N, 0:n], XPJ[:, k, :], XCS[:, k, o:o + n],
                         start=(k == 0), stop=(k == KI - 1)) for k in range(KI)][-1],
                reads=["XPJ", "XCS"], writes=[bres])
            p.on("vector", lambda v, o=o, n=n, bank=bank: v.tensor_copy(
                XPD[:, o:o + n], bank[0:R, 0:n]), reads=[bres], writes=["XP"])
            p.on("vector", lambda v, o=o, n=n, bank=bank: v.tensor_copy(
                XBC[:, o:o + n], bank[R:R + 2 * N, 0:n]), reads=[bres], writes=["XBC"])
        p.dma("sync", lambda s, hoff=hoff: s.dma_start(
            out=O_bc[:, hoff:hoff + HTC], in_=XBC[:]), reads=["XBC"])

        if stop < 6:
            continue
        for k in range(KI if stop >= 64 or stop == 8 else 1):
            buf = k % 2
            for ci, (o, n) in enumerate(CH_T):
                bank, bres = ps[2 + ci % 2], f"b{2 + ci % 2}"
                p.on("tensor", lambda t, k=k, o=o, n=n, bank=bank: t.matmul(
                    bank[:, 0:n], DTW[:, k * 128:(k + 1) * 128], XPD[:, o:o + n],
                    start=True, stop=True), reads=["DTW", "XP"], writes=[bres])
                if stop == 61:
                    p.on("vector", lambda v, o=o, n=n, bank=bank, buf=buf: v.tensor_copy(
                        DLT[:, buf, o:o + n], bank[:, 0:n]), reads=[bres], writes=[f"DLT{buf}"])
                    continue
                if stop != 63:
                    p.on("scalar", lambda s, k=k, o=o, n=n, bank=bank: s.activation(
                        SPT[:, 0:n], bank[:, 0:n], AF.Abs, bias=DTB[:, k:k + 1]),
                        reads=[bres, "DTB"], writes=["SPT"])
                    p.on("scalar", lambda s, o=o, n=n: s.activation(
                        SPT[:, 0:n], SPT[:, 0:n], AF.Exp, scale=-1.0),
                        reads=["SPT"], writes=["SPT"])
                    p.on("scalar", lambda s, o=o, n=n: s.activation(
                        SPT[:, 0:n], SPT[:, 0:n], AF.Ln, bias=1.0),
                        reads=["SPT"], writes=["SPT"])
                if stop == 62:
                    p.on("vector", lambda v, o=o, n=n, buf=buf: v.tensor_copy(
                        DLT[:, buf, o:o + n], SPT[:, 0:n]), reads=["SPT"], writes=[f"DLT{buf}"])
                    continue
                p.on("vector", lambda v, k=k, o=o, n=n, bank=bank, buf=buf: v.tensor_scalar(
                    DLT[:, buf, o:o + n], bank[:, 0:n], DTB[:, k:k + 1], 0.0, ALU.add, ALU.max),
                    reads=[bres, "DTB"], writes=[f"DLT{buf}"])
                if stop == 63:
                    continue
                if stop != 65:
                    p.on("vector", lambda v, k=k, o=o, n=n, buf=buf: v.tensor_tensor(
                        DLT[:, buf, o:o + n], DLT[:, buf, o:o + n], SPT[:, 0:n], ALU.add),
                        reads=[f"DLT{buf}", "SPT"], writes=[f"DLT{buf}"])
            p.on("vector", lambda v, k=k, buf=buf: v.tensor_tensor(
                DXC[:, buf, :], DLT[:, buf, :], XCS[:, k, :], ALU.mult),
                reads=[f"DLT{buf}", "XCS"], writes=[f"DXC{buf}"])
            if stop == 66:
                continue
            p.dma("sync", lambda s, k=k, buf=buf, hoff=hoff: s.dma_start(
                out=O_delta[k * 128:(k + 1) * 128, hoff:hoff + HTC], in_=DLT[:, buf, :]),
                reads=[f"DLT{buf}"])
            p.dma("gpsimd", lambda g, k=k, buf=buf, hoff=hoff: g.dma_start(
                out=O_dxc[k * 128:(k + 1) * 128, hoff:hoff + HTC], in_=DXC[:, buf, :]),
                reads=[f"DXC{buf}"])



def build_ascan(p, nc, stack):
    ddelta = nc.dram_tensor("ddelta", [DDC, B, T], BF16, kind="ExternalInput").ap()
    ddxc = nc.dram_tensor("ddxc", [DDC, B, T], BF16, kind="ExternalInput").ap()
    bc_in = nc.dram_tensor("bc", [B, 2 * N, T], F32, kind="ExternalInput").ap()
    A_in = nc.dram_tensor("A_log", [DDC, N], F32, kind="ExternalInput").ap()
    O_ys = nc.dram_tensor("ys", [DDC, B, T], BF16, kind="ExternalOutput").ap()

    KK = DDC // 128
    CH = 2048
    NCH = T // CH

    sb = lambda n, s, d=F32: stack.enter_context(nc.sbuf_tensor(n, s, d))
    DLT = sb("DLT", [128, 2, KK, CH], BF16)
    DXC = sb("DXC", [128, 2, KK, CH], BF16)
    BCC = sb("BCC", [2 * N, 2, CH], BF16)
    A2 = sb("A2", [128, KK, N])
    AEX = sb("AEX", [128, CH])
    BX = sb("BX", [128, CH], BF16)
    HH = sb("HH", [128, CH], BF16)
    BREP = sb("BREP", [128, CH], BF16)
    CREP = sb("CREP", [128, CH], BF16)
    YM = sb("YM", [128, CH], BF16)
    Y = sb("Y", [128, KK, CH])
    YS = sb("YS", [128, 2, KK, CH], BF16)
    CARR = sb("CARR", [128, KK, N])
    SEL = stack.enter_context(nc.sbuf_tensor("SEL", [2 * N, 2 * N * 128], BF16))

    psB = stack.enter_context(nc.psum_tensor("psB", [128, 512], F32))
    psC = stack.enter_context(nc.psum_tensor("psC", [128, 512], F32))
    p.mark_exclusive("psB", "psC")
    sel_in = nc.dram_tensor("sel", [2 * N, 2 * N * 128], BF16, kind="ExternalInput").ap()
    p.dma("sync", lambda s: s.dma_start(out=SEL[:], in_=sel_in), writes=["ONESROW"])
    p.dma("sync", lambda s: s.dma_start(
        out=A2[:], in_=A_in.rearrange("(k p) n -> p k n", p=128)), writes=["A2"])
    p.on("scalar", lambda s: s.activation(
        A2[:].rearrange("p k n -> p (k n)"), A2[:].rearrange("p k n -> p (k n)"), AF.Exp),
        reads=["A2"], writes=["A2"])
    p.on("vector", lambda v: v.tensor_scalar(
        A2[:].rearrange("p k n -> p (k n)"), A2[:].rearrange("p k n -> p (k n)"),
        -1.0, None, ALU.mult), reads=["A2"], writes=["A2"])

    for b in range(B):
        for ci in range(NCH):
            gi = b * NCH + ci
            buf = gi % 2
            o = ci * CH
            p.dma("sync", lambda s, b=b, o=o, buf=buf: s.dma_start(
                out=DLT[:, buf], in_=ddelta[:, b, o:o + CH].rearrange("(k p) t -> p k t", p=128)),
                writes=[f"DLT{buf}"])
            p.dma("scalar", lambda s, b=b, o=o, buf=buf: s.dma_start(
                out=DXC[:, buf], in_=ddxc[:, b, o:o + CH].rearrange("(k p) t -> p k t", p=128)),
                writes=[f"DXC{buf}"])
            p.dma("gpsimd", lambda g, b=b, o=o, buf=buf: g.dma_start(
                out=BCC[:, buf, :], in_=bc_in[b, :, o:o + CH]),
                writes=[f"BCC{buf}"])
            for kk in range(KK):
                for n in range(N):
                    for (ob, nb) in chunks(CH, 512):
                        p.on("tensor", lambda t, n=n, buf=buf, ob=ob, nb=nb: t.matmul(
                            psB[:, 0:nb], SEL[:, n * 128:(n + 1) * 128],
                            BCC[:, buf, ob:ob + nb], start=True, stop=True),
                            reads=[f"BCC{buf}", "ONESROW"], writes=["psB"])
                        p.on("scalar", lambda sc, ob=ob, nb=nb: sc.activation(
                            BREP[:, ob:ob + nb], psB[:, 0:nb], AF.Copy),
                            reads=["psB"], writes=["BREP"])
                        p.on("tensor", lambda t, n=n, buf=buf, ob=ob, nb=nb: t.matmul(
                            psC[:, 0:nb], SEL[:, (N + n) * 128:(N + n + 1) * 128],
                            BCC[:, buf, ob:ob + nb], start=True, stop=True),
                            reads=[f"BCC{buf}", "ONESROW"], writes=["psC"])
                        p.on("scalar", lambda sc, ob=ob, nb=nb: sc.activation(
                            CREP[:, ob:ob + nb], psC[:, 0:nb], AF.Copy),
                            reads=["psC"], writes=["CREP"])
                    p.on("scalar", lambda s, kk=kk, n=n, buf=buf: s.activation(
                        AEX[:], DLT[:, buf, kk, :], AF.Exp, scale=A2[:, kk, n:n + 1]),
                        reads=[f"DLT{buf}", "A2"], writes=["AEX"])
                    p.on("vector", lambda v, kk=kk, buf=buf: v.tensor_tensor(
                        BX[:], DXC[:, buf, kk, :], BREP[:], ALU.mult),
                        reads=[f"DXC{buf}", "BREP"], writes=["BX"])
                    if ci == 0:
                        p.on("vector", lambda v: v.tensor_tensor_scan(
                            HH[:], AEX[:], BX[:], 0.0, ALU.mult, ALU.add),
                            reads=["AEX", "BX"], writes=["HH"])
                    else:
                        p.on("vector", lambda v, kk=kk, n=n: v.tensor_tensor_scan(
                            HH[:], AEX[:], BX[:], CARR[:, kk, n:n + 1], ALU.mult, ALU.add),
                            reads=["AEX", "BX", "CARR"], writes=["HH"])
                    p.on("vector", lambda v, kk=kk, n=n: v.tensor_copy(
                        CARR[:, kk, n:n + 1], HH[:, CH - 1:CH]),
                        reads=["HH"], writes=["CARR"])
                    p.on("vector", lambda v: v.tensor_tensor(
                        YM[:], HH[:], CREP[:], ALU.mult),
                        reads=["HH", "CREP"], writes=["YM"])
                    if n == 0:
                        p.on("vector", lambda v, kk=kk: v.tensor_copy(Y[:, kk, :], YM[:]),
                             reads=["YM"], writes=["Y"])
                    else:
                        p.on("vector", lambda v, kk=kk: v.tensor_tensor(
                            Y[:, kk, :], Y[:, kk, :], YM[:], ALU.add),
                            reads=["YM", "Y"], writes=["Y"])
                p.on("vector", lambda v, kk=kk, buf=buf: v.tensor_copy(
                    YS[:, buf, kk, :], Y[:, kk, :]),
                    reads=["Y"], writes=[f"YS{buf}"])
            p.dma("sync", lambda s, b=b, o=o, buf=buf: s.dma_start(
                out=O_ys[:, b, o:o + CH].rearrange("(k p) t -> p k t", p=128),
                in_=YS[:, buf]), reads=[f"YS{buf}"])



def build_apost(p, nc, stack, final=False):
    KD, KI = D // 128, DI // 128
    HTC = TC // 2
    ysT = nc.dram_tensor("ysT", [DI, TC], BF16, kind="ExternalInput").ap()
    xcs = nc.dram_tensor("xcs", [DI, TC], BF16, kind="ExternalInput").ap()
    zs = nc.dram_tensor("zs", [DI, TC], BF16, kind="ExternalInput").ap()
    out_wT = nc.dram_tensor("out_wT", [DI, D], F32, kind="ExternalInput").ap()
    Dp = nc.dram_tensor("Dp", [DI, 1], F32, kind="ExternalInput").ap()
    xa_in = nc.dram_tensor("xa_in", [D, TC], F32, kind="ExternalInput").ap()
    O_xa = nc.dram_tensor("xa_out", [D, TC], F32, kind="ExternalOutput").ap()
    if final:
        fw = nc.dram_tensor("final_w", [D, 1], F32, kind="ExternalInput").ap()
        x_in = nc.dram_tensor("x_in", [D, TC], F32, kind="ExternalInput").ap()
        O_x = nc.dram_tensor("x_out", [D, TC], F32, kind="ExternalOutput").ap()

    sb = lambda n, s_, d=F32: stack.enter_context(nc.sbuf_tensor(n, s_, d))
    Ys = sb("Ysb", [128, KI, HTC], BF16)
    Xc = sb("Xcb", [128, KI, HTC], BF16)
    Zs = sb("Zsb", [128, KI, HTC], BF16)
    Yv = Ys
    DPc = sb("DPc", [128, KI])
    WO = sb("WO", [128, 2, KI, 128], BF16)
    XA = sb("XA", [128, KD, HTC])
    ROWS = sb("ROWS", [1, HTC])
    BRD = sb("BRD", [128, HTC])
    SQ = sb("SQ", [128, HTC])
    ONE = sb("ONE", [128, 1])
    ONR = sb("ONR", [1, 128])
    if final:
        XI = sb("XI", [128, KD, HTC])
        FW = sb("FW", [128, KD])
    ps = [stack.enter_context(nc.psum_tensor(f"ps{i}", [128, 512], F32)) for i in range(4)]
    psS = stack.enter_context(nc.psum_tensor("psS", [128, 512], F32))

    cst = Consts(p, nc, stack)
    p.mark_exclusive("b0", "b1", "b2", "b3", "psS")
    p.dma("sync", lambda s: s.dma_start(out=DPc[:], in_=Dp.rearrange("(k p) o -> p (k o)", p=128), allow_slow_non_contiguous=True), writes=["DPc"])
    if final:
        p.dma("sync", lambda s: s.dma_start(out=FW[:], in_=fw.rearrange("(k p) o -> p (k o)", p=128), allow_slow_non_contiguous=True), writes=["FW"])
    p.on("vector", lambda v: v.memset(ONE[:], 1.0), writes=["ONE"])
    p.on("vector", lambda v: v.memset(ONR[:], 1.0), writes=["ONESROW"])

    for half in range(2):
        ho = half * HTC
        p.dma("sync", lambda s, ho=ho: s.dma_start(
            out=Ys[:], in_=ysT[:, ho:ho + HTC].rearrange("(k p) t -> p k t", p=128)), writes=["Ys"])
        p.dma("scalar", lambda s, ho=ho: s.dma_start(
            out=Xc[:], in_=xcs[:, ho:ho + HTC].rearrange("(k p) t -> p k t", p=128)), writes=["Xc"])
        p.dma("gpsimd", lambda g, ho=ho: g.dma_start(
            out=Zs[:], in_=zs[:, ho:ho + HTC].rearrange("(k p) t -> p k t", p=128)),
            writes=["Zs"])
        p.dma("scalar", lambda s, ho=ho: s.dma_start(
            out=XA[:], in_=xa_in[:, ho:ho + HTC].rearrange("(k p) t -> p k t", p=128)), writes=["XA"])
        if final:
            p.dma("gpsimd", lambda g, ho=ho: g.dma_start(
                out=XI[:], in_=x_in[:, ho:ho + HTC].rearrange("(k p) t -> p k t", p=128)),
                writes=["XI"])

        for k in range(KI):
            p.on("vector", lambda v, k=k: v.scalar_tensor_tensor(
                Yv[:, k, :], Xc[:, k, :], DPc[:, k:k + 1], Ys[:, k, :], ALU.mult, ALU.add),
                reads=["Xc", "Ys", "DPc"], writes=["Ys", "Yv"])
            p.on("vector", lambda v, k=k: v.tensor_tensor(
                Yv[:, k, :], Yv[:, k, :], Zs[:, k, :], ALU.mult),
                reads=["Yv", "Zs"], writes=["Yv"])

        CH_T = chunks(HTC, 512)
        for dk in range(KD):
            buf = dk % 2
            p.dma("gpsimd", lambda g, dk=dk, buf=buf: g.dma_start(
                out=WO[:, buf], in_=out_wT[:, dk * 128:(dk + 1) * 128].rearrange("(k p) f -> p k f", p=128)),
                writes=[f"WO{buf}"])
            for ci, (o, n) in enumerate(CH_T):
                bank, bres = ps[ci % 4], f"b{ci % 4}"
                p.on("tensor", lambda t, o=o, n=n, bank=bank, buf=buf: [
                    t.matmul(bank[:, 0:n], WO[:, buf, k, :], Yv[:, k, o:o + n],
                             start=(k == 0), stop=(k == KI - 1)) for k in range(KI)][-1],
                    reads=[f"WO{buf}", "Yv"], writes=[bres])
                p.on("vector", lambda v, dk=dk, o=o, n=n, bank=bank: v.tensor_tensor(
                    XA[:, dk, o:o + n], XA[:, dk, o:o + n], bank[:, 0:n], ALU.add),
                    reads=[bres, "XA"], writes=["XA"])
        p.dma("sync", lambda s, ho=ho: s.dma_start(
            out=O_xa[:, ho:ho + HTC].rearrange("(k p) t -> p k t", p=128), in_=XA[:]), reads=["XA"])

        if final:
            for (o, n) in chunks(HTC, 512):
                for k in range(KD):
                    p.on("scalar", lambda s, k=k, o=o, n=n: s.activation(
                        SQ[:, 0:n], XA[:, k, o:o + n], AF.Square), reads=["XA"], writes=["SQk"])
                    p.on("tensor", lambda t, k=k, o=o, n=n: t.matmul(
                        psS[0:1, 0:n], ONE[:], SQ[:, 0:n], start=(k == 0), stop=(k == KD - 1)),
                        reads=["SQk", "ONE"], writes=["psS"])
                p.on("vector", lambda v, o=o, n=n: v.tensor_copy(
                    ROWS[0:1, o:o + n], psS[0:1, 0:n]), reads=["psS"], writes=["r"])
            p.on("vector", lambda v: v.tensor_scalar(ROWS[:], ROWS[:], 1.0 / D, None, ALU.mult),
                 reads=["r"], writes=["r"])
            rsqrt_row(p, cst, ROWS[:], "r", RMS_EPS)
            pe_bcast(p, ROWS[:], "r", BRD[:], "BRD", psS, "psS", ONR[:], HTC)
            for k in range(KD):
                p.on("vector", lambda v, k=k: v.tensor_tensor(SQ[:], XA[:, k, :], BRD[:], ALU.mult),
                     reads=["XA", "BRD"], writes=["SQk"])
                p.on("vector", lambda v, k=k: v.tensor_scalar(SQ[:], SQ[:], FW[:, k:k + 1], None, ALU.mult),
                     reads=["SQk", "FW"], writes=["SQk"])
                p.on("vector", lambda v, k=k: v.tensor_tensor(XI[:, k, :], XI[:, k, :], SQ[:], ALU.add),
                     reads=["XI", "SQk"], writes=["XI"])
            p.dma("sync", lambda s, ho=ho: s.dma_start(
                out=O_x[:, ho:ho + HTC].rearrange("(k p) t -> p k t", p=128), in_=XI[:]), reads=["XI"])



def emit_quant(p, nc, tag, src_ap, src_res, out_ap, out_res, scr, F):
    G16 = F // 16
    MX, INV, SC, XN, VA, SG, QA = (scr[k] for k in ("MX", "INV", "SC", "XN", "VA", "SG", "QA"))
    p.on("vector", lambda v: v.tensor_reduce(
        MX[:, 0:G16].rearrange("p (g o) -> p g o", o=1),
        src_ap.rearrange("p (g i) -> p g i", i=16),
        AXX, ALU.max, apply_absolute_value=True),
        reads=[src_res], writes=[tag + "MX"])
    p.on("vector", lambda v: v.tensor_scalar(
        SC[:, 0:G16], MX[:, 0:G16], 1.0 / 6.0, 1e-12, ALU.mult, ALU.max),
        reads=[tag + "MX"], writes=[tag + "SC"])
    p.on("vector", lambda v: v.reciprocal(INV[:, 0:G16], SC[:, 0:G16]),
        reads=[tag + "SC"], writes=[tag + "INV"])
    p.on("vector", lambda v: v.tensor_tensor(
        XN[:, 0:F].rearrange("p (g i) -> p g i", i=16),
        src_ap.rearrange("p (g i) -> p g i", i=16),
        INV[:, 0:G16].rearrange("p (g o) -> p g o", o=1).broadcast_to([128, G16, 16]),
        ALU.mult), reads=[src_res, tag + "INV"], writes=[tag + "XN"])
    p.on("scalar", lambda sc_: sc_.activation(VA[:, 0:F], XN[:, 0:F], AF.Abs),
        reads=[tag + "XN"], writes=[tag + "VA"])
    p.on("scalar", lambda sc_: sc_.activation(SG[:, 0:F], XN[:, 0:F], AF.Sign),
        reads=[tag + "XN"], writes=[tag + "SG"])
    p.on("vector", lambda v: v.tensor_scalar(QA[:, 0:F], VA[:, 0:F], 0.25, None, ALU.is_gt),
        reads=[tag + "VA"], writes=[tag + "QA"])
    for thr in (0.75, 1.25, 1.75):
        p.on("vector", lambda v, thr=thr: v.scalar_tensor_tensor(
            QA[:, 0:F], VA[:, 0:F], thr, QA[:, 0:F], ALU.is_gt, ALU.add),
            reads=[tag + "VA", tag + "QA"], writes=[tag + "QA"])
    p.on("vector", lambda v: v.tensor_scalar(QA[:, 0:F], QA[:, 0:F], 0.5, None, ALU.mult),
        reads=[tag + "QA"], writes=[tag + "QA"])
    for thr in (2.5, 3.5, 5.0, 5.0):
        p.on("vector", lambda v, thr=thr: v.scalar_tensor_tensor(
            QA[:, 0:F], VA[:, 0:F], thr, QA[:, 0:F], ALU.is_gt, ALU.add),
            reads=[tag + "VA", tag + "QA"], writes=[tag + "QA"])
    p.on("vector", lambda v: v.tensor_tensor(
        VA[:, 0:F].rearrange("p (g i) -> p g i", i=16),
        SG[:, 0:F].rearrange("p (g i) -> p g i", i=16),
        SC[:, 0:G16].rearrange("p (g o) -> p g o", o=1).broadcast_to([128, G16, 16]),
        ALU.mult), reads=[tag + "SG", tag + "SC"], writes=[tag + "VA"])
    p.on("vector", lambda v: v.tensor_tensor(out_ap, QA[:, 0:F], VA[:, 0:F], ALU.mult),
        reads=[tag + "QA", tag + "VA"], writes=[out_res])


def mk_quant_scratch(nc, stack, name, F):
    sb = lambda n, s_, d=F32: stack.enter_context(nc.sbuf_tensor(n, s_, d))
    return {"MX": sb(name + "MX", [128, F // 16]),
            "INV": sb(name + "INV", [128, F // 16]),
            "SC": sb(name + "SC", [128, F // 16]),
            "XN": sb(name + "XN", [128, F]),
            "VA": sb(name + "VA", [128, F]),
            "SG": sb(name + "SG", [128, F], BF16),
            "QA": sb(name + "QA", [128, F], BF16)}



def build_wprep(p, nc, stack):
    wg = nc.dram_tensor("w_gate", [FF, D], F32, kind="ExternalInput").ap()
    wu = nc.dram_tensor("w_up", [FF, D], F32, kind="ExternalInput").ap()
    wd = nc.dram_tensor("w_down", [D, FF], F32, kind="ExternalInput").ap()
    eye = nc.dram_tensor("eye", [128, 128], BF16, kind="ExternalInput").ap()
    O_g = nc.dram_tensor("wgT", [D, FF], BF16, kind="ExternalOutput").ap()
    O_u = nc.dram_tensor("wuT", [D, FF], BF16, kind="ExternalOutput").ap()
    O_d = nc.dram_tensor("wdT", [FF, D], BF16, kind="ExternalOutput").ap()

    KF = FF // 128
    KD = D // 128
    sb = lambda n, s_, d=F32: stack.enter_context(nc.sbuf_tensor(n, s_, d))
    WROW = sb("WROW", [128, 2, FF])
    WQ = sb("WQ", [128, FF], BF16)
    EYE = stack.enter_context(nc.sbuf_tensor("EYE", [128, 128], BF16))
    scr = mk_quant_scratch(nc, stack, "q", FF)
    pst = stack.enter_context(nc.psum_tensor("pst", [128, 128], BF16))
    pstF = stack.enter_context(nc.psum_tensor("pstF", [128, 128], F32))
    p.mark_exclusive("pst")
    p.dma("sync", lambda s: s.dma_start(out=EYE[:], in_=eye), writes=["EYE"])

    WTG = sb("WTG", [128, KD, FF], BF16)
    WTD = sb("WTD", [128, KF, D], BF16)

    def do_w2(w_in, KT, FW, out_hbm, hold, hres, oname):
        for kt in range(KT):
            buf = kt % 2
            p.dma("gpsimd", lambda g, kt=kt, buf=buf, w_in=w_in, FW=FW: g.dma_start(
                out=WROW[:, buf, 0:FW], in_=w_in[kt * 128:(kt + 1) * 128, :]),
                writes=[f"WROW{buf}"], channel=oname + f"l{buf}")
            emit_quant(p, nc, "q", WROW[:, buf, 0:FW], f"WROW{buf}", WQ[:, 0:FW], "WQ", scr, FW)
            for j in range(FW // 128):
                p.on("tensor", lambda t, j=j: t.transpose(
                    pst[:], WQ[:, j * 128:(j + 1) * 128], EYE[:]),
                    reads=["WQ", "EYE"], writes=["pst"])
                p.on("vector", lambda v, j=j, kt=kt, hold=hold: v.tensor_copy(
                    hold[:, j, kt * 128:(kt + 1) * 128], pst[:]),
                    reads=["pst"], writes=[hres])
        p.dma("sync", lambda s, out_hbm=out_hbm, hold=hold: s.dma_start(
            out=out_hbm.rearrange("(k p) f -> p k f", p=128), in_=hold[:]),
            reads=[hres], channel="st" + oname)

    do_w2(wg, KF, D, O_g, WTG, "WTG", "g")
    do_w2(wu, KF, D, O_u, WTG, "WTG", "u")
    do_w2(wd, KD, FF, O_d, WTD, "WTD", "d")



def build_ffn(p, nc, stack):
    QT = 256
    KD, KF = D // 128, FF // 128
    x_in = nc.dram_tensor("x_in", [D, TC], F32, kind="ExternalInput").ap()
    wgT = nc.dram_tensor("wgT", [D, FF], BF16, kind="ExternalInput").ap()
    wuT = nc.dram_tensor("wuT", [D, FF], BF16, kind="ExternalInput").ap()
    wdT = nc.dram_tensor("wdT", [FF, D], BF16, kind="ExternalInput").ap()
    eye = nc.dram_tensor("eye", [128, 128], BF16, kind="ExternalInput").ap()
    O_x = nc.dram_tensor("x_out", [D, TC], F32, kind="ExternalOutput").ap()

    sb = lambda n, s_, d=F32: stack.enter_context(nc.sbuf_tensor(n, s_, d))
    X = sb("X", [128, KD, QT])
    XLN = sb("XLN", [128, KD, QT])
    XT = sb("XT", [128, QT // 128, D])
    XQ = sb("XQ", [128, QT // 128, D], BF16)
    XQT = sb("XQT", [128, KD, QT], BF16)
    GS = sb("GS", [128, KF, QT])
    US = sb("US", [128, KF, QT])
    HQT = sb("HQT", [128, KF, QT], BF16)
    HT = sb("HT", [128, FF])
    HQt = sb("HQt", [128, FF], BF16)
    WG2 = sb("WG2", [128, 2, KD, 128], BF16)
    WU2 = sb("WU2", [128, 2, KD, 128], BF16)
    WD2 = sb("WD2", [128, 2, KF, 128], BF16)
    EYE = stack.enter_context(nc.sbuf_tensor("EYE", [128, 128], BF16))
    ROWS = sb("ROWS", [1, 2 * QT])
    BRD = sb("BRD", [128, 2 * QT])
    SQ1 = sb("SQ1", [128, 512])
    ONE = sb("ONE", [128, 1])
    ONR = sb("ONR", [1, 128])
    scr = mk_quant_scratch(nc, stack, "f", FF)
    cst = Consts(p, nc, stack)

    ps = [stack.enter_context(nc.psum_tensor(f"ps{i}", [128, 512], F32)) for i in range(4)]
    psS = stack.enter_context(nc.psum_tensor("psS", [128, 512], F32))
    pst = stack.enter_context(nc.psum_tensor("pst", [128, 128], BF16))
    pstF = stack.enter_context(nc.psum_tensor("pstF", [128, 128], F32))
    p.mark_exclusive("b0", "b1", "b2", "b3", "psS", "pst", "pstF")

    EYEF = stack.enter_context(nc.sbuf_tensor("EYEF", [128, 128], F32))
    p.dma("sync", lambda s: s.dma_start(out=EYE[:], in_=eye), writes=["EYE"])
    p.on("vector", lambda v: v.tensor_copy(EYEF[:], EYE[:]), reads=["EYE"], writes=["EYEF"])
    p.on("vector", lambda v: v.memset(ONE[:], 1.0), writes=["ONE"])
    p.on("vector", lambda v: v.memset(ONR[:], 1.0), writes=["ONESROW"])

    sx = ROWS[:, 0:QT]
    r0 = ROWS[:, QT:2 * QT]
    brd0 = BRD[:, 0:QT]
    brd1 = BRD[:, QT:2 * QT]

    for q in range(TC // QT):
        qo = q * QT
        p.dma("sync", lambda s, qo=qo: s.dma_start(
            out=X[:], in_=x_in[:, qo:qo + QT].rearrange("(k p) t -> p k t", p=128)),
            writes=["X"])
        for k in range(KD):
            p.on("scalar", lambda s, k=k: s.activation(SQ1[:, 0:QT], X[:, k, :], AF.Square),
                 reads=["X"], writes=["SQ1"])
            p.on("tensor", lambda t, k=k: t.matmul(
                psS[0:1, 0:QT], ONE[:], SQ1[:, 0:QT], start=(k == 0), stop=(k == KD - 1)),
                reads=["SQ1", "ONE"], writes=["psS"])
        p.on("vector", lambda v: v.tensor_copy(r0, psS[0:1, 0:QT]), reads=["psS"], writes=["r0"])
        for k in range(KD):
            p.on("tensor", lambda t, k=k: t.matmul(
                psS[0:1, 0:QT], ONE[:], X[:, k, :], start=(k == 0), stop=(k == KD - 1)),
                reads=["X", "ONE"], writes=["psS"])
        p.on("vector", lambda v: v.tensor_copy(sx, psS[0:1, 0:QT]), reads=["psS"], writes=["sx"])
        p.on("vector", lambda v: v.tensor_scalar(sx, sx, 1.0 / D, None, ALU.mult),
             reads=["sx"], writes=["sx"])
        p.on("vector", lambda v: v.tensor_scalar(r0, r0, 1.0 / D, None, ALU.mult),
             reads=["r0"], writes=["r0"])
        p.on("vector", lambda v: v.tensor_tensor(BRD[0:1, 0:QT], sx, sx, ALU.mult),
             reads=["sx"], writes=["BRDt"])
        p.on("vector", lambda v: v.tensor_tensor(r0, r0, BRD[0:1, 0:QT], ALU.subtract),
             reads=["r0", "BRDt"], writes=["r0"])
        rsqrt_row(p, cst, r0, "r0", LNE_OUT)
        p.on("vector", lambda v: v.tensor_tensor(sx, sx, r0, ALU.mult),
             reads=["sx", "r0"], writes=["sx"])
        pe_bcast(p, r0, "r0", brd0, "BRD0", psS, "psS", ONR[:], QT)
        pe_bcast(p, sx, "sx", brd1, "BRD1", psS, "psS", ONR[:], QT)
        for k in range(KD):
            p.on("vector", lambda v, k=k: v.tensor_tensor(
                XLN[:, k, :], X[:, k, :], brd0, ALU.mult), reads=["X", "BRD0"], writes=["XLN"])
            p.on("vector", lambda v, k=k: v.tensor_tensor(
                XLN[:, k, :], XLN[:, k, :], brd1, ALU.subtract), reads=["XLN", "BRD1"], writes=["XLN"])
        for tt in range(QT // 128):
            for dk in range(KD):
                p.on("tensor", lambda t, tt=tt, dk=dk: t.transpose(
                    pstF[:], XLN[:, dk, tt * 128:(tt + 1) * 128], EYEF[:]),
                    reads=["XLN", "EYEF"], writes=["pstF"])
                p.on("vector", lambda v, tt=tt, dk=dk: v.tensor_copy(
                    XT[:, tt, dk * 128:(dk + 1) * 128], pstF[:]), reads=["pstF"], writes=["XT"])
        for tt in range(QT // 128):
            emit_quant(p, nc, "f", XT[:, tt, :], "XT", XQ[:, tt, :], "XQ", scr, D)
        for tt in range(QT // 128):
            for dk in range(KD):
                p.on("tensor", lambda t, tt=tt, dk=dk: t.transpose(
                    pst[:], XQ[:, tt, dk * 128:(dk + 1) * 128], EYE[:]),
                    reads=["XQ", "EYE"], writes=["pst"])
                p.on("vector", lambda v, tt=tt, dk=dk: v.tensor_copy(
                    XQT[:, dk, tt * 128:(tt + 1) * 128], pst[:]), reads=["pst"], writes=["XQT"])
        for e in range(KF):
            buf = e % 2
            p.dma("gpsimd", lambda g, e=e, buf=buf: g.dma_start(
                out=WG2[:, buf], in_=wgT[:, e * 128:(e + 1) * 128].rearrange("(k p) f -> p k f", p=128)),
                writes=[f"WG2{buf}"])
            p.dma("scalar", lambda g, e=e, buf=buf: g.dma_start(
                out=WU2[:, buf], in_=wuT[:, e * 128:(e + 1) * 128].rearrange("(k p) f -> p k f", p=128)),
                writes=[f"WU2{buf}"])
            p.on("tensor", lambda t, e=e, buf=buf: [
                t.matmul(ps[0][:, 0:QT], WG2[:, buf, k, :], XQT[:, k, :],
                         start=(k == 0), stop=(k == KD - 1)) for k in range(KD)][-1],
                reads=[f"WG2{buf}", "XQT"], writes=["b0"])
            p.on("scalar", lambda s, e=e: s.activation(GS[:, e, :], ps[0][:, 0:QT], AF.Silu),
                 reads=["b0"], writes=["GS"])
            p.on("tensor", lambda t, e=e, buf=buf: [
                t.matmul(ps[1][:, 0:QT], WU2[:, buf, k, :], XQT[:, k, :],
                         start=(k == 0), stop=(k == KD - 1)) for k in range(KD)][-1],
                reads=[f"WU2{buf}", "XQT"], writes=["b1"])
            p.on("vector", lambda v, e=e: v.tensor_copy(US[:, e, :], ps[1][:, 0:QT]),
                 reads=["b1"], writes=["US"])
            p.on("vector", lambda v, e=e: v.tensor_tensor(
                GS[:, e, :], GS[:, e, :], US[:, e, :], ALU.mult),
                reads=["GS", "US"], writes=["GS"])
        for tt in range(QT // 128):
            for e in range(KF):
                p.on("tensor", lambda t, tt=tt, e=e: t.transpose(
                    pstF[:], GS[:, e, tt * 128:(tt + 1) * 128], EYEF[:]),
                    reads=["GS", "EYEF"], writes=["pstF"])
                p.on("vector", lambda v, tt=tt, e=e: v.tensor_copy(
                    HT[:, e * 128:(e + 1) * 128], pstF[:]), reads=["pstF"], writes=["HT"])
            emit_quant(p, nc, "f", HT[:], "HT", HQt[:], "HQt", scr, FF)
            for e in range(KF):
                p.on("tensor", lambda t, tt=tt, e=e: t.transpose(
                    pst[:], HQt[:, e * 128:(e + 1) * 128], EYE[:]),
                    reads=["HQt", "EYE"], writes=["pst"])
                p.on("vector", lambda v, tt=tt, e=e: v.tensor_copy(
                    HQT[:, e, tt * 128:(tt + 1) * 128], pst[:]), reads=["pst"], writes=["HQT"])
        for dk in range(KD):
            buf = dk % 2
            p.dma("gpsimd", lambda g, dk=dk, buf=buf: g.dma_start(
                out=WD2[:, buf], in_=wdT[:, dk * 128:(dk + 1) * 128].rearrange("(k p) f -> p k f", p=128)),
                writes=[f"WD2{buf}"])
            p.on("tensor", lambda t, dk=dk, buf=buf: [
                t.matmul(ps[2][:, 0:QT], WD2[:, buf, k, :], HQT[:, k, :],
                         start=(k == 0), stop=(k == KF - 1)) for k in range(KF)][-1],
                reads=[f"WD2{buf}", "HQT"], writes=["b2"])
            p.on("vector", lambda v, dk=dk: v.tensor_tensor(
                X[:, dk, :], X[:, dk, :], ps[2][:, 0:QT], ALU.add),
                reads=["b2", "X"], writes=["X"])
        p.dma("sync", lambda s, qo=qo: s.dma_start(
            out=O_x[:, qo:qo + QT].rearrange("(k p) t -> p k t", p=128), in_=X[:]),
            reads=["X"], channel="st_X")



def _ln_gb(p, nc, cst, X, XRES, HL, G2, B2v, ROWS, BRD, SQ1, psS, ONE, ONR, KD, W, eps):
    Xres = XRES
    sx = ROWS[:, 0:W]
    r0 = ROWS[:, W:2 * W]
    for (o, n) in chunks(W, 512):
        for k in range(KD):
            p.on("scalar", lambda s_, k=k, o=o, n=n: s_.activation(
                SQ1[:, 0:n], X[:, k, o:o + n], AF.Square), reads=[Xres], writes=["SQ1"])
            p.on("tensor", lambda t, k=k, o=o, n=n: t.matmul(
                psS[0:1, 0:n], ONE[:], SQ1[:, 0:n], start=(k == 0), stop=(k == KD - 1)),
                reads=["SQ1", "ONE"], writes=["psS"])
        p.on("vector", lambda v, o=o, n=n: v.tensor_copy(r0[0:1, o:o + n], psS[0:1, 0:n]),
             reads=["psS"], writes=["r0"])
        p.on("tensor", lambda t, o=o, n=n: [
            t.matmul(psS[0:1, 0:n], ONE[:], X[:, k, o:o + n],
                     start=(k == 0), stop=(k == KD - 1)) for k in range(KD)][-1],
            reads=[Xres, "ONE"], writes=["psS"])
        p.on("vector", lambda v, o=o, n=n: v.tensor_copy(sx[0:1, o:o + n], psS[0:1, 0:n]),
             reads=["psS"], writes=["sx"])
    p.on("vector", lambda v: v.tensor_scalar(sx, sx, 1.0 / D, None, ALU.mult),
         reads=["sx"], writes=["sx"])
    p.on("vector", lambda v: v.tensor_scalar(r0, r0, 1.0 / D, None, ALU.mult),
         reads=["r0"], writes=["r0"])
    p.on("vector", lambda v: v.tensor_tensor(BRD[0:1, 0:W], sx, sx, ALU.mult),
         reads=["sx"], writes=["BRDt"])
    p.on("vector", lambda v: v.tensor_tensor(r0, r0, BRD[0:1, 0:W], ALU.subtract),
         reads=["r0", "BRDt"], writes=["r0"])
    rsqrt_row(p, cst, r0, "r0", eps)
    p.on("vector", lambda v: v.tensor_tensor(sx, sx, r0, ALU.mult),
         reads=["sx", "r0"], writes=["sx"])
    pe_bcast(p, r0, "r0", BRD[:, 0:W], "BRD0", psS, "psS", ONR[:], W)
    pe_bcast(p, sx, "sx", BRD[:, W:2 * W], "BRD1", psS, "psS", ONR[:], W)
    for k in range(KD):
        p.on("vector", lambda v, k=k: v.tensor_tensor(
            HL[:, k, :], X[:, k, :], BRD[:, 0:W], ALU.mult), reads=[Xres, "BRD0"], writes=["HL"])
        p.on("vector", lambda v, k=k: v.tensor_tensor(
            HL[:, k, :], HL[:, k, :], BRD[:, W:2 * W], ALU.subtract),
            reads=["HL", "BRD1"], writes=["HL"])
        p.on("vector", lambda v, k=k: v.tensor_scalar(
            HL[:, k, :], HL[:, k, :], G2[:, k:k + 1], B2v[:, k:k + 1], ALU.mult, ALU.add),
            reads=["HL", "G2v", "B2v"], writes=["HL"])


def _ln_plain(p, nc, cst, X, xres, OUT, ores, ROWS, BRD, SQ1, psS, ONE, ONR, KD, W, eps):
    sx = ROWS[:, 0:W]
    r0 = ROWS[:, W:2 * W]
    for (o, n) in chunks(W, 512):
        for k in range(KD):
            p.on("scalar", lambda s_, k=k, o=o, n=n: s_.activation(
                SQ1[:, 0:n], X[:, k, o:o + n], AF.Square), reads=[xres], writes=["SQ1"])
            p.on("tensor", lambda t, k=k, o=o, n=n: t.matmul(
                psS[0:1, 0:n], ONE[:], SQ1[:, 0:n], start=(k == 0), stop=(k == KD - 1)),
                reads=["SQ1", "ONE"], writes=["psS"])
        p.on("vector", lambda v, o=o, n=n: v.tensor_copy(r0[0:1, o:o + n], psS[0:1, 0:n]),
             reads=["psS"], writes=["r0"])
        p.on("tensor", lambda t, o=o, n=n: [
            t.matmul(psS[0:1, 0:n], ONE[:], X[:, k, o:o + n],
                     start=(k == 0), stop=(k == KD - 1)) for k in range(KD)][-1],
            reads=[xres, "ONE"], writes=["psS"])
        p.on("vector", lambda v, o=o, n=n: v.tensor_copy(sx[0:1, o:o + n], psS[0:1, 0:n]),
             reads=["psS"], writes=["sx"])
    p.on("vector", lambda v: v.tensor_scalar(sx, sx, 1.0 / D, None, ALU.mult),
         reads=["sx"], writes=["sx"])
    p.on("vector", lambda v: v.tensor_scalar(r0, r0, 1.0 / D, None, ALU.mult),
         reads=["r0"], writes=["r0"])
    p.on("vector", lambda v: v.tensor_tensor(BRD[0:1, 0:W], sx, sx, ALU.mult),
         reads=["sx"], writes=["BRDt"])
    p.on("vector", lambda v: v.tensor_tensor(r0, r0, BRD[0:1, 0:W], ALU.subtract),
         reads=["r0", "BRDt"], writes=["r0"])
    rsqrt_row(p, cst, r0, "r0", eps)
    p.on("vector", lambda v: v.tensor_tensor(sx, sx, r0, ALU.mult),
         reads=["sx", "r0"], writes=["sx"])
    pe_bcast(p, r0, "r0", BRD[:, 0:W], "BRD0", psS, "psS", ONR[:], W)
    pe_bcast(p, sx, "sx", BRD[:, W:2 * W], "BRD1", psS, "psS", ONR[:], W)
    for k in range(KD):
        p.on("vector", lambda v, k=k: v.tensor_tensor(
            OUT[:, k, :], X[:, k, :], BRD[:, 0:W], ALU.mult), reads=[xres, "BRD0"], writes=[ores])
        p.on("vector", lambda v, k=k: v.tensor_tensor(
            OUT[:, k, :], OUT[:, k, :], BRD[:, W:2 * W], ALU.subtract),
            reads=[ores, "BRD1"], writes=[ores])


def build_b1(p, nc, stack):
    KD = D // 128
    HTC = TC // 2
    HW_ = 1 + HTC
    xh = nc.dram_tensor("xh", [D, HALO + TC], F32, kind="ExternalInput").ap()
    Wk = nc.dram_tensor("Wk", [D, D], F32, kind="ExternalInput").ap()
    Wv = nc.dram_tensor("Wv", [D, D], F32, kind="ExternalInput").ap()
    Wr = nc.dram_tensor("Wr", [D, D], F32, kind="ExternalInput").ap()
    g1 = nc.dram_tensor("ln1_g", [D, 1], F32, kind="ExternalInput").ap()
    b1_ = nc.dram_tensor("ln1_b", [D, 1], F32, kind="ExternalInput").ap()
    mk_ = nc.dram_tensor("mix_k", [D, 1], F32, kind="ExternalInput").ap()
    mv_ = nc.dram_tensor("mix_v", [D, 1], F32, kind="ExternalInput").ap()
    mr_ = nc.dram_tensor("mix_r", [D, 1], F32, kind="ExternalInput").ap()
    td_ = nc.dram_tensor("td", [D, 1], F32, kind="ExternalInput").ap()
    tf_ = nc.dram_tensor("tf", [D, 1], F32, kind="ExternalInput").ap()
    hm_ = nc.dram_tensor("hmask", [D, 1], F32, kind="ExternalInput").ap()
    O_xl = nc.dram_tensor("xl", [D, TC], F32, kind="ExternalOutput").ap()
    O_al = nc.dram_tensor("alph", [D, TC], F32, kind="ExternalOutput").ap()
    O_be = nc.dram_tensor("bet", [D, TC], F32, kind="ExternalOutput").ap()
    O_euk = nc.dram_tensor("euk", [D, TC], BF16, kind="ExternalOutput").ap()
    O_vv = nc.dram_tensor("vv", [D, TC], BF16, kind="ExternalOutput").ap()
    O_rr = nc.dram_tensor("rr", [D, TC], BF16, kind="ExternalOutput").ap()

    sb = lambda n, s_, d=F32: stack.enter_context(nc.sbuf_tensor(n, s_, d))
    X = sb("Xb", [128, KD, HW_])
    XL = sb("XLb", [128, KD, HW_])
    HL = sb("HL", [128, KD, HW_])
    XK = sb("XK", [128, KD, HTC], BF16)
    XV = sb("XV", [128, KD, HTC], BF16)
    XR = sb("XR", [128, KD, HTC], BF16)
    HS = sb("HS", [128, HTC])
    WS = sb("WSb", [128, 2, KD, 128], BF16)
    EKt = sb("EKt", [128, HTC], BF16)
    EUKt = sb("EUKt", [128, HTC], BF16)
    VVt = sb("VVt", [128, HTC], BF16)
    RRt = sb("RRt", [128, HTC], BF16)
    AVV = sb("AVV", [128, HTC], BF16)
    ALPHt = sb("ALPHt", [128, HTC])
    BETt = sb("BETt", [128, HTC])
    ACAR = sb("ACAR", [128, KD])
    BCAR = sb("BCAR", [128, KD])
    VEC = sb("VEC", [128, 8 * KD])
    EXPW = sb("EXPW", [128, KD])
    EWREP = sb("EWREP", [128, KD, HTC], BF16)
    ONES = sb("ONESb", [128, HTC], BF16)
    ROWS = sb("ROWSb", [1, 2 * HW_])
    BRD = sb("BRDb", [128, 2 * HW_])
    SQ1 = sb("SQ1b", [128, 512])
    ONE = sb("ONEb", [128, 1])
    ONR = sb("ONRb", [1, 128])
    cst = Consts(p, nc, stack)
    ps = [stack.enter_context(nc.psum_tensor(f"ps{i}", [128, 512], F32)) for i in range(3)]
    psS = stack.enter_context(nc.psum_tensor("psSb", [128, 512], F32))
    p.mark_exclusive("b0", "b1", "b2", "psS")

    for i, t in enumerate([g1, b1_, mk_, mv_, mr_, td_, tf_, hm_]):
        p.dma("sync", lambda s, i=i, t=t: s.dma_start(
            out=VEC[:, i * KD:(i + 1) * KD],
            in_=t.rearrange("(k p) o -> p (k o)", p=128), allow_slow_non_contiguous=True),
            writes=["VEC"], channel=f"vec{i}")
    G2 = VEC[:, 0:KD]; B2v = VEC[:, KD:2 * KD]
    MIXS = [VEC[:, 2 * KD:3 * KD], VEC[:, 3 * KD:4 * KD], VEC[:, 4 * KD:5 * KD]]
    TD = VEC[:, 5 * KD:6 * KD]; TF = VEC[:, 6 * KD:7 * KD]; HM = VEC[:, 7 * KD:8 * KD]
    p.on("vector", lambda v: v.memset(ONE[:], 1.0), writes=["ONE"])
    p.on("vector", lambda v: v.memset(ONR[:], 1.0), writes=["ONESROW"])
    p.on("vector", lambda v: v.memset(ONES[:], 1.0), writes=["ONES"])
    p.on("scalar", lambda s_: s_.activation(EXPW[:], TD, AF.Exp), reads=["VEC"], writes=["EXPW"])
    p.on("vector", lambda v: v.tensor_scalar(EXPW[:], EXPW[:], -1.0, None, ALU.mult),
         reads=["EXPW"], writes=["EXPW"])
    p.on("scalar", lambda s_: s_.activation(EXPW[:], EXPW[:], AF.Exp), reads=["EXPW"], writes=["EXPW"])
    for k in range(KD):
        p.on("vector", lambda v, k=k: v.tensor_scalar(
            EWREP[:, k, :], ONES[:], EXPW[:, k:k + 1], None, ALU.mult),
            reads=["ONES", "EXPW"], writes=["EWREP"])

    for half in range(2):
        ho = half * HTC
        p.dma("sync", lambda s, ho=ho: s.dma_start(
            out=X[:], in_=xh[:, HALO - 1 + ho:HALO + ho + HTC].rearrange("(k p) t -> p k t", p=128)),
            writes=["Xb"])
        _ln_plain(p, nc, cst, X, "Xb", XL, "XLb", ROWS, BRD, SQ1, psS, ONE, ONR, KD, HW_, LNE_OUT)
        p.dma("gpsimd", lambda s, ho=ho: s.dma_start(
            out=O_xl[:, ho:ho + HTC].rearrange("(k p) t -> p k t", p=128), in_=XL[:, :, 1:]),
            reads=["XLb"], channel="stxl")
        _ln_gb(p, nc, cst, XL, "XLb", HL, G2, B2v, ROWS, BRD, SQ1, psS, ONE, ONR, KD, HW_, LNE_B)
        if half == 0:
            for k in range(KD):
                p.on("vector", lambda v, k=k: v.tensor_scalar(
                    HL[:, k, 0:1], HL[:, k, 0:1], HM[:, k:k + 1], None, ALU.mult),
                    reads=["HL", "VEC"], writes=["HL"])
        for k in range(KD):
            p.on("vector", lambda v, k=k: v.tensor_tensor(
                HS[:], HL[:, k, 1:HW_], HL[:, k, 0:HTC], ALU.subtract),
                reads=["HL"], writes=["HSr"])
            for m, XM, xres in ((0, XK, "XKr"), (1, XV, "XVr"), (2, XR, "XRr")):
                p.on("vector", lambda v, k=k, m=m, XM=XM: v.scalar_tensor_tensor(
                    XM[:, k, :], HS[:], MIXS[m][:, k:k + 1], HL[:, k, 0:HTC],
                    ALU.mult, ALU.add), reads=["HSr", "VEC", "HL"], writes=[xres])
        CH_T = chunks(HTC, 512)
        for ht in range(KD):
            buf = ht % 2
            wres = f"WSb{buf}"
            for wi, (W_in, XM, xres) in enumerate(((Wk, XK, "XKr"), (Wv, XV, "XVr"), (Wr, XR, "XRr"))):
                p.dma("gpsimd", lambda g, W_in=W_in, ht=ht, buf=buf: g.dma_start(
                    out=WS[:, buf], in_=W_in[:, ht * 128:(ht + 1) * 128].rearrange("(k p) f -> p k f", p=128)),
                    writes=[wres], channel=f"wsb{buf}")
                for ci, (o, n) in enumerate(CH_T):
                    bank, bres = ps[ci % 2], f"b{ci % 2}"
                    p.on("tensor", lambda t, o=o, n=n, bank=bank, buf=buf, XM=XM: [
                        t.matmul(bank[:, 0:n], WS[:, buf, k, :], XM[:, k, o:o + n],
                                 start=(k == 0), stop=(k == KD - 1)) for k in range(KD)][-1],
                        reads=[wres, xres], writes=[bres])
                    if wi == 0:
                        p.on("scalar", lambda s_, o=o, n=n, bank=bank: s_.activation(
                            EKt[:, o:o + n], bank[:, 0:n], AF.Exp),
                            reads=[bres], writes=["EKt"])
                        p.on("scalar", lambda s_, o=o, n=n, bank=bank, ht=ht: s_.activation(
                            EUKt[:, o:o + n], bank[:, 0:n], AF.Exp, bias=TF[:, ht:ht + 1]),
                            reads=[bres, "VEC"], writes=["EUKt"])
                    elif wi == 1:
                        p.on("vector", lambda v, o=o, n=n, bank=bank: v.tensor_copy(
                            VVt[:, o:o + n], bank[:, 0:n]), reads=[bres], writes=["VVt"])
                    else:
                        p.on("vector", lambda v, o=o, n=n, bank=bank: v.tensor_copy(
                            RRt[:, o:o + n], bank[:, 0:n]), reads=[bres], writes=["RRt"])
            p.on("vector", lambda v: v.tensor_tensor(AVV[:], EKt[:], VVt[:], ALU.mult),
                 reads=["EKt", "VVt"], writes=["AVV"])
            if half == 0:
                p.on("vector", lambda v, ht=ht: v.tensor_tensor_scan(
                    ALPHt[:], EWREP[:, ht, :], AVV[:], 0.0, ALU.mult, ALU.add),
                    reads=["EWREP", "AVV"], writes=["ALPHt"])
                p.on("vector", lambda v, ht=ht: v.tensor_tensor_scan(
                    BETt[:], EWREP[:, ht, :], EKt[:], 0.0, ALU.mult, ALU.add),
                    reads=["EWREP", "EKt"], writes=["BETt"])
            else:
                p.on("vector", lambda v, ht=ht: v.tensor_tensor_scan(
                    ALPHt[:], EWREP[:, ht, :], AVV[:], ACAR[:, ht:ht + 1], ALU.mult, ALU.add),
                    reads=["EWREP", "AVV", "ACAR"], writes=["ALPHt"])
                p.on("vector", lambda v, ht=ht: v.tensor_tensor_scan(
                    BETt[:], EWREP[:, ht, :], EKt[:], BCAR[:, ht:ht + 1], ALU.mult, ALU.add),
                    reads=["EWREP", "EKt", "BCAR"], writes=["BETt"])
            if half == 0:
                p.on("vector", lambda v, ht=ht: v.tensor_copy(
                    ACAR[:, ht:ht + 1], ALPHt[:, HTC - 1:HTC]), reads=["ALPHt"], writes=["ACAR"])
                p.on("vector", lambda v, ht=ht: v.tensor_copy(
                    BCAR[:, ht:ht + 1], BETt[:, HTC - 1:HTC]), reads=["BETt"], writes=["BCAR"])
            p.dma("scalar", lambda s, ht=ht, ho=ho: s.dma_start(
                out=O_euk[ht * 128:(ht + 1) * 128, ho:ho + HTC], in_=EUKt[:]),
                reads=["EUKt"], channel="steuk")
            p.dma("scalar", lambda s, ht=ht, ho=ho: s.dma_start(
                out=O_vv[ht * 128:(ht + 1) * 128, ho:ho + HTC], in_=VVt[:]),
                reads=["VVt"], channel="stvv")
            p.dma("gpsimd", lambda g, ht=ht, ho=ho: g.dma_start(
                out=O_rr[ht * 128:(ht + 1) * 128, ho:ho + HTC], in_=RRt[:]),
                reads=["RRt"], channel="strr")
            p.dma("sync", lambda s, ht=ht, ho=ho: s.dma_start(
                out=O_al[ht * 128:(ht + 1) * 128, ho:ho + HTC], in_=ALPHt[:]),
                reads=["ALPHt"], channel="stal")
            p.dma("sync", lambda s, ht=ht, ho=ho: s.dma_start(
                out=O_be[ht * 128:(ht + 1) * 128, ho:ho + HTC], in_=BETt[:]),
                reads=["BETt"], channel="stbe")


def build_b2(p, nc, stack):
    KD = D // 128
    HTC = TC // 2
    alph = nc.dram_tensor("alph", [D, TC], F32, kind="ExternalInput").ap()
    bet = nc.dram_tensor("bet", [D, TC], F32, kind="ExternalInput").ap()
    euk = nc.dram_tensor("euk", [D, TC], BF16, kind="ExternalInput").ap()
    vv = nc.dram_tensor("vv", [D, TC], BF16, kind="ExternalInput").ap()
    rr = nc.dram_tensor("rr", [D, TC], BF16, kind="ExternalInput").ap()
    Wo = nc.dram_tensor("Wo", [D, D], F32, kind="ExternalInput").ap()
    x_in = nc.dram_tensor("x_in", [D, TC], F32, kind="ExternalInput").ap()
    td_ = nc.dram_tensor("td", [D, 1], F32, kind="ExternalInput").ap()
    ae_ = nc.dram_tensor("ae", [D, 3], F32, kind="ExternalInput").ap()
    be_ = nc.dram_tensor("be", [D, 3], F32, kind="ExternalInput").ap()
    ramp = nc.dram_tensor("ramp", [1, TC], F32, kind="ExternalInput").ap()
    O_x = nc.dram_tensor("x_out", [D, TC], F32, kind="ExternalOutput").ap()

    sb = lambda n, s_, d=F32: stack.enter_context(nc.sbuf_tensor(n, s_, d))
    X = sb("Xb2", [128, KD, HTC])
    AL = sb("AL", [128, HTC + 1])
    BE = sb("BE", [128, HTC + 1])
    EU = sb("EU", [128, HTC], BF16)
    VT = sb("VT", [128, HTC], BF16)
    RT = sb("RT", [128, HTC], BF16)
    PP = sb("PP", [128, HTC])
    NUM = sb("NUM", [128, HTC])
    DEN = sb("DEN", [128, HTC])
    MM = sb("MMb", [128, KD, HTC], BF16)
    WS = sb("WSb2", [128, 2, KD, 128], BF16)
    VEC = sb("VECb2", [128, 8 * KD])
    RAMPR = sb("RAMPR", [128, TC])
    ONR = sb("ONRb2", [1, 128])
    ps = [stack.enter_context(nc.psum_tensor(f"ps{i}", [128, 512], F32)) for i in range(2)]
    psS = stack.enter_context(nc.psum_tensor("psSb2", [128, 512], F32))
    p.mark_exclusive("b0", "b1", "psS")

    p.dma("sync", lambda s: s.dma_start(
        out=VEC[:, 0:KD], in_=td_.rearrange("(k p) o -> p (k o)", p=128),
        allow_slow_non_contiguous=True), writes=["VEC"], channel="vtd")
    p.dma("sync", lambda s: s.dma_start(
        out=VEC[:, KD:4 * KD].rearrange("p (k o) -> p k o", o=3),
        in_=ae_.rearrange("(k p) o -> p k o", p=128),
        allow_slow_non_contiguous=True), writes=["VEC"], channel="vae")
    p.dma("sync", lambda s: s.dma_start(
        out=VEC[:, 4 * KD:7 * KD].rearrange("p (k o) -> p k o", o=3),
        in_=be_.rearrange("(k p) o -> p k o", p=128),
        allow_slow_non_contiguous=True), writes=["VEC"], channel="vbe")
    TD = VEC[:, 0:KD]
    AE = VEC[:, KD:4 * KD].rearrange("p (k o) -> p k o", o=3)
    BEv = VEC[:, 4 * KD:7 * KD].rearrange("p (k o) -> p k o", o=3)
    WCOL = sb("WCOL", [128, KD])
    DTOT = sb("DTOT", [128, 3 * KD])
    AIN = sb("AIN", [128, KD])
    BIN = sb("BIN", [128, KD])
    PP2 = sb("PP2", [128, KD])
    RMPROW = sb("RMPROW", [1, TC])
    p.dma("scalar", lambda s: s.dma_start(out=RMPROW[:], in_=ramp), writes=["rmp"])
    p.on("vector", lambda v: v.memset(ONR[:], 1.0), writes=["ONESROW"])
    for (o, n) in chunks(TC, 512):
        p.on("tensor", lambda t, o=o, n=n: t.matmul(
            psS[:, 0:n], ONR[:], RMPROW[0:1, o:o + n], start=True, stop=True),
            reads=["rmp", "ONESROW"], writes=["psS"])
        p.on("vector", lambda v, o=o, n=n: v.tensor_copy(RAMPR[:, o:o + n], psS[:, 0:n]),
             reads=["psS"], writes=["RAMPR"])
    p.on("scalar", lambda s_: s_.activation(WCOL[:], TD, AF.Exp), reads=["VEC"], writes=["WCOL"])
    p.on("vector", lambda v: v.tensor_scalar(WCOL[:], WCOL[:], -1.0, None, ALU.mult),
         reads=["WCOL"], writes=["WCOL"])
    p.on("vector", lambda v: v.tensor_scalar(DTOT[:, 0:KD], WCOL[:], float(TC), None, ALU.mult),
         reads=["WCOL"], writes=["DTOT"])
    p.on("scalar", lambda s_: s_.activation(DTOT[:, 0:KD], DTOT[:, 0:KD], AF.Exp),
         reads=["DTOT"], writes=["DTOT"])
    p.on("vector", lambda v: v.tensor_tensor(DTOT[:, KD:2 * KD], DTOT[:, 0:KD], DTOT[:, 0:KD], ALU.mult),
         reads=["DTOT"], writes=["DTOT"])
    for (OUT, ores, SRC) in ((AIN, "AIN", AE), (BIN, "BIN", BEv)):
        p.on("vector", lambda v, OUT=OUT, SRC=SRC: v.tensor_copy(OUT[:], SRC[:, :, 0]),
             reads=["VEC"], writes=[ores])
        p.on("vector", lambda v, OUT=OUT, SRC=SRC: v.scalar_tensor_tensor(
            PP2[:], SRC[:, :, 1], 1.0, DTOT[:, 0:KD], ALU.mult, ALU.mult),
            reads=["VEC", "DTOT"], writes=["PPs"])
        p.on("vector", lambda v, OUT=OUT: v.tensor_tensor(OUT[:], OUT[:], PP2[:], ALU.add),
             reads=[ores, "PPs"], writes=[ores])
        p.on("vector", lambda v, OUT=OUT, SRC=SRC: v.scalar_tensor_tensor(
            PP2[:], SRC[:, :, 2], 1.0, DTOT[:, KD:2 * KD], ALU.mult, ALU.mult),
            reads=["VEC", "DTOT"], writes=["PPs"])
        p.on("vector", lambda v, OUT=OUT: v.tensor_tensor(OUT[:], OUT[:], PP2[:], ALU.add),
             reads=[ores, "PPs"], writes=[ores])

    CH_T = chunks(HTC, 512)
    for half in range(2):
        ho = half * HTC
        p.dma("sync", lambda s, ho=ho: s.dma_start(
            out=X[:], in_=x_in[:, ho:ho + HTC].rearrange("(k p) t -> p k t", p=128)),
            writes=["Xb2"])
        for ht in range(KD):
            if half == 0:
                p.on("vector", lambda v: v.memset(AL[:, 0:1], 0.0), writes=["AL"])
                p.on("vector", lambda v: v.memset(BE[:, 0:1], 0.0), writes=["BE"])
                p.dma("sync", lambda s, ht=ht: s.dma_start(
                    out=AL[:, 1:HTC], in_=alph[ht * 128:(ht + 1) * 128, 0:HTC - 1]),
                    writes=["AL"], channel="lal")
                p.dma("sync", lambda s, ht=ht: s.dma_start(
                    out=BE[:, 1:HTC], in_=bet[ht * 128:(ht + 1) * 128, 0:HTC - 1]),
                    writes=["BE"], channel="lbe")
            else:
                p.dma("sync", lambda s, ht=ht, ho=ho: s.dma_start(
                    out=AL[:, 0:HTC], in_=alph[ht * 128:(ht + 1) * 128, ho - 1:ho + HTC - 1]),
                    writes=["AL"], channel="lal")
                p.dma("sync", lambda s, ht=ht, ho=ho: s.dma_start(
                    out=BE[:, 0:HTC], in_=bet[ht * 128:(ht + 1) * 128, ho - 1:ho + HTC - 1]),
                    writes=["BE"], channel="lbe")
            p.dma("scalar", lambda s, ht=ht, ho=ho: s.dma_start(
                out=EU[:], in_=euk[ht * 128:(ht + 1) * 128, ho:ho + HTC]),
                writes=["EU"], channel="leu")
            p.dma("scalar", lambda s, ht=ht, ho=ho: s.dma_start(
                out=VT[:], in_=vv[ht * 128:(ht + 1) * 128, ho:ho + HTC]),
                writes=["VT"], channel="lvt")
            p.dma("gpsimd", lambda g, ht=ht, ho=ho: g.dma_start(
                out=RT[:], in_=rr[ht * 128:(ht + 1) * 128, ho:ho + HTC]),
                writes=["RT"], channel="lrt")
            p.on("scalar", lambda s_, ht=ht, ho=ho: s_.activation(
                PP[:], RAMPR[:, ho:ho + HTC], AF.Exp, scale=WCOL[:, ht:ht + 1]),
                reads=["RAMPR", "WCOL"], writes=["PPb"])
            p.on("vector", lambda v, ht=ht: v.scalar_tensor_tensor(
                AL[:, 0:HTC], PP[:], AIN[:, ht:ht + 1], AL[:, 0:HTC], ALU.mult, ALU.add),
                reads=["PPb", "AIN", "AL"], writes=["AL"])
            p.on("vector", lambda v, ht=ht: v.scalar_tensor_tensor(
                BE[:, 0:HTC], PP[:], BIN[:, ht:ht + 1], BE[:, 0:HTC], ALU.mult, ALU.add),
                reads=["PPb", "BIN", "BE"], writes=["BE"])
            p.on("vector", lambda v: v.tensor_tensor(NUM[:], EU[:], VT[:], ALU.mult),
                 reads=["EU", "VT"], writes=["NUMr"])
            p.on("vector", lambda v: v.tensor_tensor(NUM[:], NUM[:], AL[:, 0:HTC], ALU.add),
                 reads=["NUMr", "AL"], writes=["NUMr"])
            p.on("vector", lambda v: v.tensor_tensor(DEN[:], BE[:, 0:HTC], EU[:], ALU.add),
                 reads=["BE", "EU"], writes=["DENr"])
            p.on("vector", lambda v: v.reciprocal(DEN[:], DEN[:]), reads=["DENr"], writes=["DENr"])
            p.on("vector", lambda v: v.tensor_tensor(NUM[:], NUM[:], DEN[:], ALU.mult),
                 reads=["NUMr", "DENr"], writes=["NUMr"])
            p.on("scalar", lambda s_: s_.activation(DEN[:], RT[:], AF.Sigmoid),
                 reads=["RT"], writes=["DENr"])
            p.on("vector", lambda v, ht=ht: v.tensor_tensor(
                MM[:, ht, :], NUM[:], DEN[:], ALU.mult),
                reads=["NUMr", "DENr"], writes=["MMr"])
        for dk in range(KD):
            buf = dk % 2
            p.dma("gpsimd", lambda g, dk=dk, buf=buf: g.dma_start(
                out=WS[:, buf], in_=Wo[:, dk * 128:(dk + 1) * 128].rearrange("(k p) f -> p k f", p=128)),
                writes=[f"WSb2{buf}"], channel=f"wo{buf}")
            for ci, (o, n) in enumerate(CH_T):
                bank, bres = ps[ci % 2], f"b{ci % 2}"
                p.on("tensor", lambda t, o=o, n=n, bank=bank, buf=buf: [
                    t.matmul(bank[:, 0:n], WS[:, buf, k, :], MM[:, k, o:o + n],
                             start=(k == 0), stop=(k == KD - 1)) for k in range(KD)][-1],
                    reads=[f"WSb2{buf}", "MMr"], writes=[bres])
                p.on("vector", lambda v, dk=dk, o=o, n=n, bank=bank: v.tensor_tensor(
                    X[:, dk, o:o + n], X[:, dk, o:o + n], bank[:, 0:n], ALU.add),
                    reads=[bres, "Xb2"], writes=["Xb2"])
        p.dma("sync", lambda s, ho=ho: s.dma_start(
            out=O_x[:, ho:ho + HTC].rearrange("(k p) t -> p k t", p=128), in_=X[:]),
            reads=["Xb2"], channel="stx")


def build_b3(p, nc, stack):
    KD = D // 128
    KC4 = 4 * D // 128
    QT = 512
    HW_ = 1 + QT
    xh = nc.dram_tensor("xh", [D, HALO + TC], F32, kind="ExternalInput").ap()
    WK4 = nc.dram_tensor("cm_Wk", [D, 4 * D], F32, kind="ExternalInput").ap()
    WV4 = nc.dram_tensor("cm_Wv", [4 * D, D], F32, kind="ExternalInput").ap()
    WR1 = nc.dram_tensor("cm_Wr", [D, D], F32, kind="ExternalInput").ap()
    g2 = nc.dram_tensor("ln2_g", [D, 1], F32, kind="ExternalInput").ap()
    b2_ = nc.dram_tensor("ln2_b", [D, 1], F32, kind="ExternalInput").ap()
    mk_ = nc.dram_tensor("mix_k", [D, 1], F32, kind="ExternalInput").ap()
    mr_ = nc.dram_tensor("mix_r", [D, 1], F32, kind="ExternalInput").ap()
    hm_ = nc.dram_tensor("hmask", [D, 1], F32, kind="ExternalInput").ap()
    xo_ = nc.dram_tensor("x_orig", [D, TC], F32, kind="ExternalInput").ap()
    O_x = nc.dram_tensor("x_out", [D, TC], F32, kind="ExternalOutput").ap()

    sb = lambda n, s_, d=F32: stack.enter_context(nc.sbuf_tensor(n, s_, d))
    X = sb("Xb", [128, KD, HW_])
    XO = sb("XO", [128, KD, QT])
    HL = sb("HL", [128, KD, HW_])
    XK = sb("XK", [128, KD, QT], BF16)
    XR = sb("XR", [128, KD, QT], BF16)
    HS = sb("HS", [128, QT])
    KCs = sb("KCs", [128, KC4, QT], BF16)
    RC = sb("RC", [128, KD, QT], BF16)
    WS = sb("WSb3", [128, 2, KD, 128], BF16)
    WS4 = sb("WS4", [128, 2, KC4, 128], BF16)
    VEC = sb("VECb3", [128, 5 * KD])
    SQ1 = sb("SQ1b3", [128, 512])
    ROWS = sb("ROWSb3", [1, 2 * HW_])
    BRD = sb("BRDb3", [128, 2 * HW_])
    ONE = sb("ONEb3", [128, 1])
    ONR = sb("ONRb3", [1, 128])
    cst = Consts(p, nc, stack)
    ps = [stack.enter_context(nc.psum_tensor(f"ps{i}", [128, 512], F32)) for i in range(3)]
    psS = stack.enter_context(nc.psum_tensor("psSb3", [128, 512], F32))
    p.mark_exclusive("b0", "b1", "b2", "psS")

    for i, t in enumerate([g2, b2_, mk_, mr_, hm_]):
        p.dma("sync", lambda s, i=i, t=t: s.dma_start(
            out=VEC[:, i * KD:(i + 1) * KD],
            in_=t.rearrange("(k p) o -> p (k o)", p=128), allow_slow_non_contiguous=True),
            writes=["VEC"], channel=f"vec{i}")
    G2 = VEC[:, 0:KD]; B2v = VEC[:, KD:2 * KD]
    MK = VEC[:, 2 * KD:3 * KD]; MR = VEC[:, 3 * KD:4 * KD]; HM = VEC[:, 4 * KD:5 * KD]
    p.on("vector", lambda v: v.memset(ONE[:], 1.0), writes=["ONE"])
    p.on("vector", lambda v: v.memset(ONR[:], 1.0), writes=["ONESROW"])

    for q in range(TC // QT):
        qo = q * QT
        p.dma("sync", lambda s, qo=qo: s.dma_start(
            out=X[:], in_=xh[:, HALO - 1 + qo:HALO + qo + QT].rearrange("(k p) t -> p k t", p=128)),
            writes=["Xb"])
        p.dma("scalar", lambda s, qo=qo: s.dma_start(
            out=XO[:], in_=xo_[:, qo:qo + QT].rearrange("(k p) t -> p k t", p=128)),
            writes=["XO"])
        _ln_gb(p, nc, cst, X, "Xb", HL, G2, B2v, ROWS, BRD, SQ1, psS, ONE, ONR, KD, HW_, LNE_B)
        if q == 0:
            for k in range(KD):
                p.on("vector", lambda v, k=k: v.tensor_scalar(
                    HL[:, k, 0:1], HL[:, k, 0:1], HM[:, k:k + 1], None, ALU.mult),
                    reads=["HL", "VEC"], writes=["HL"])
        for k in range(KD):
            p.on("vector", lambda v, k=k: v.tensor_tensor(
                HS[:], HL[:, k, 1:HW_], HL[:, k, 0:QT], ALU.subtract),
                reads=["HL"], writes=["HSr"])
            p.on("vector", lambda v, k=k: v.scalar_tensor_tensor(
                XK[:, k, :], HS[:], MK[:, k:k + 1], HL[:, k, 0:QT], ALU.mult, ALU.add),
                reads=["HSr", "VEC", "HL"], writes=["XKr"])
            p.on("vector", lambda v, k=k: v.scalar_tensor_tensor(
                XR[:, k, :], HS[:], MR[:, k:k + 1], HL[:, k, 0:QT], ALU.mult, ALU.add),
                reads=["HSr", "VEC", "HL"], writes=["XRr"])
        for e in range(KC4):
            buf = e % 2
            p.dma("gpsimd", lambda g, e=e, buf=buf: g.dma_start(
                out=WS[:, buf], in_=WK4[:, e * 128:(e + 1) * 128].rearrange("(k p) f -> p k f", p=128)),
                writes=[f"WSb3{buf}"], channel=f"w4{buf}")
            p.on("tensor", lambda t, e=e, buf=buf: [
                t.matmul(ps[0][:, 0:QT], WS[:, buf, k, :], XK[:, k, :],
                         start=(k == 0), stop=(k == KD - 1)) for k in range(KD)][-1],
                reads=[f"WSb3{buf}", "XKr"], writes=["b0"])
            p.on("vector", lambda v, e=e: v.tensor_scalar(
                SQ1[:, 0:QT], ps[0][:, 0:QT], 0.0, None, ALU.max),
                reads=["b0"], writes=["SQ1"])
            p.on("scalar", lambda s_, e=e: s_.activation(
                KCs[:, e, :], SQ1[:, 0:QT], AF.Square), reads=["SQ1"], writes=["KCr"])
        for e in range(KD):
            buf = e % 2
            p.dma("gpsimd", lambda g, e=e, buf=buf: g.dma_start(
                out=WS[:, buf], in_=WR1[:, e * 128:(e + 1) * 128].rearrange("(k p) f -> p k f", p=128)),
                writes=[f"WSb3{buf}"], channel=f"wr{buf}")
            p.on("tensor", lambda t, e=e, buf=buf: [
                t.matmul(ps[1][:, 0:QT], WS[:, buf, k, :], XR[:, k, :],
                         start=(k == 0), stop=(k == KD - 1)) for k in range(KD)][-1],
                reads=[f"WSb3{buf}", "XRr"], writes=["b1"])
            p.on("scalar", lambda s_, e=e: s_.activation(
                RC[:, e, :], ps[1][:, 0:QT], AF.Sigmoid), reads=["b1"], writes=["RCr"])
        for dk in range(KD):
            buf = dk % 2
            p.dma("gpsimd", lambda g, dk=dk, buf=buf: g.dma_start(
                out=WS4[:, buf], in_=WV4[:, dk * 128:(dk + 1) * 128].rearrange("(k p) f -> p k f", p=128)),
                writes=[f"WS4{buf}"], channel=f"wv{buf}")
            p.on("tensor", lambda t, dk=dk, buf=buf: [
                t.matmul(ps[2][:, 0:QT], WS4[:, buf, k, :], KCs[:, k, :],
                         start=(k == 0), stop=(k == KC4 - 1)) for k in range(KC4)][-1],
                reads=[f"WS4{buf}", "KCr"], writes=["b2"])
            p.on("vector", lambda v, dk=dk: v.tensor_tensor(
                SQ1[:, 0:QT], ps[2][:, 0:QT], RC[:, dk, :], ALU.mult),
                reads=["b2", "RCr"], writes=["SQ1"])
            p.on("vector", lambda v, dk=dk: v.tensor_tensor(
                X[:, dk, 1:1 + QT], X[:, dk, 1:1 + QT], SQ1[:, 0:QT], ALU.add),
                reads=["Xb", "SQ1"], writes=["Xb"])
            p.on("vector", lambda v, dk=dk: v.tensor_tensor(
                X[:, dk, 1:1 + QT], X[:, dk, 1:1 + QT], XO[:, dk, :], ALU.add),
                reads=["Xb", "XO"], writes=["Xb"])
        p.dma("sync", lambda s, qo=qo: s.dma_start(
            out=O_x[:, qo:qo + QT].rearrange("(k p) t -> p k t", p=128), in_=X[:, :, 1:1 + QT]),
            reads=["Xb"], channel="stx")



import ml_dtypes


def make_sel():
    sel = np.zeros((2 * N, 2 * N * 128), ml_dtypes.bfloat16)
    for j in range(2 * N):
        sel[j, j * 128:(j + 1) * 128] = 1
    return sel


def run_attn_a(xT, w):
    ph_pre0 = Phase("apre_ln", build_apre, outer_ln=True)
    ph_pre = Phase("apre", build_apre, outer_ln=False)
    ph_scan = Phase("ascan", build_ascan)
    ph_post = Phase("apost", build_apost, final=False)
    ph_postf = Phase("apost_f", build_apost, final=True)
    sel = make_sel()

    xa = None
    for l in range(NL):
        pre = ph_pre0 if l == 0 else ph_pre
        shards = shard_halo(xT if l == 0 else xa)
        in_maps = [{"xh": shards[k], "in_wT": w["in_wT"][l], "xprojT": w["xprojT"][l],
                    "dt_wT": w["dt_wT"][l], "conv_w": w["conv_w"][l],
                    "conv_b": w["conv_b"][l], "dt_b": w["dt_b"][l],
                    "norm_g": w["norm_w"][l]} for k in range(NCORES)]
        res = pre.run(in_maps)
        delta = unshard([np.asarray(r["delta"]) for r in res], B, DI, T)
        dxc = unshard([np.asarray(r["dxc"]) for r in res], B, DI, T)
        bc = unshard([np.asarray(r["bc"]) for r in res], B, 2 * N, T)
        xcs = [np.asarray(r["xcs"]) for r in res]
        zs = [np.asarray(r["zs"]) for r in res]
        if l == 0:
            xa = unshard([np.asarray(r["xa"]) for r in res], B, D, T)

        in_maps = []
        for c in range(NCORES):
            s0, s1 = c * DDC, (c + 1) * DDC
            in_maps.append({
                "sel": sel,
                "ddelta": np.ascontiguousarray(delta[:, s0:s1].transpose(1, 0, 2)),
                "ddxc": np.ascontiguousarray(dxc[:, s0:s1].transpose(1, 0, 2)),
                "bc": np.ascontiguousarray(bc),
                "A_log": np.ascontiguousarray(w["A_log"][l][s0:s1]),
            })
        res = ph_scan.run(in_maps)
        ysf = np.zeros((B, DI, T), ml_dtypes.bfloat16)
        for c in range(NCORES):
            ysf[:, c * DDC:(c + 1) * DDC] = np.asarray(res[c]["ys"]).transpose(1, 0, 2)

        final = (l == NL - 1)
        ph = ph_postf if final else ph_post
        in_maps = []
        for k in range(NCORES):
            b, j = divmod(k, 4)
            t0 = j * TC
            m = {"ysT": np.ascontiguousarray(ysf[b, :, t0:t0 + TC]),
                 "xcs": xcs[k], "zs": zs[k],
                 "out_wT": w["out_wT"][l], "Dp": w["Dp"][l],
                 "xa_in": np.ascontiguousarray(xa[b, :, t0:t0 + TC])}
            if final:
                m["final_w"] = w["final_w"]
                m["x_in"] = np.ascontiguousarray(xT[b, :, t0:t0 + TC])
            in_maps.append(m)
        res = ph.run(in_maps)
        if final:
            return unshard([np.asarray(r["x_out"]) for r in res], B, D, T)
        xa = unshard([np.asarray(r["xa_out"]) for r in res], B, D, T)
    return None


def run_ffn(xT, wq, eye):
    ph = Phase("ffn", build_ffn)
    in_maps = []
    for k in range(NCORES):
        b, j = divmod(k, 4)
        t0 = j * TC
        in_maps.append({"x_in": np.ascontiguousarray(xT[b, :, t0:t0 + TC]),
                        "wgT": wq["wgT"], "wuT": wq["wuT"], "wdT": wq["wdT"], "eye": eye})
    res = ph.run(in_maps)
    return unshard([np.asarray(r["x_out"]) for r in res], B, D, T)


def run_attn_b(xT, w):
    ph1 = Phase("b1", build_b1)
    ph2 = Phase("b2", build_b2)
    ph3 = Phase("b3", build_b3)
    col = lambda v: np.ascontiguousarray(np.asarray(v, np.float32).reshape(D, 1))
    shards = shard_halo(xT)
    in_maps = []
    for k in range(NCORES):
        j = k % 4
        hm = np.zeros((D, 1), np.float32) if j == 0 else np.ones((D, 1), np.float32)
        in_maps.append({"xh": shards[k], "Wk": w["tm_Wk"], "Wv": w["tm_Wv"], "Wr": w["tm_Wr"],
                        "ln1_g": col(w["ln1_g"]), "ln1_b": col(w["ln1_b"]),
                        "mix_k": col(w["tm_mix_k"]), "mix_v": col(w["tm_mix_v"]),
                        "mix_r": col(w["tm_mix_r"]), "td": col(w["time_decay"]),
                        "tf": col(w["time_first"]), "hmask": hm})
    r1 = ph1.run(in_maps)
    alph = [np.asarray(r["alph"]) for r in r1]
    bet = [np.asarray(r["bet"]) for r in r1]
    xl = [np.asarray(r["xl"]) for r in r1]
    ramp = np.ascontiguousarray(np.arange(TC, dtype=np.float32).reshape(1, TC))
    zcol = np.zeros((D,), np.float32)
    in_maps = []
    for k in range(NCORES):
        b, j = divmod(k, 4)
        t0 = j * TC
        ae = np.stack([alph[b * 4 + (j - 1 - i)][:, TC - 1] if j - 1 - i >= 0 else zcol
                       for i in range(3)], axis=1)
        be = np.stack([bet[b * 4 + (j - 1 - i)][:, TC - 1] if j - 1 - i >= 0 else zcol
                       for i in range(3)], axis=1)
        in_maps.append({"alph": alph[k], "bet": bet[k],
                        "euk": np.asarray(r1[k]["euk"]), "vv": np.asarray(r1[k]["vv"]),
                        "rr": np.asarray(r1[k]["rr"]), "Wo": w["Wo"],
                        "x_in": xl[k],
                        "td": col(w["time_decay"]),
                        "ae": np.ascontiguousarray(ae), "be": np.ascontiguousarray(be),
                        "ramp": ramp})
    r2 = ph2.run(in_maps)
    x2 = unshard([np.asarray(r["x_out"]) for r in r2], B, D, T)
    shards = shard_halo(x2)
    in_maps = []
    for k in range(NCORES):
        j = k % 4
        hm = np.zeros((D, 1), np.float32) if j == 0 else np.ones((D, 1), np.float32)
        b, j = divmod(k, 4)
        t0 = j * TC
        in_maps.append({"xh": shards[k], "cm_Wk": w["cm_Wk"], "cm_Wv": w["cm_Wv"],
                        "cm_Wr": w["cm_Wr"], "ln2_g": col(w["ln2_g"]), "ln2_b": col(w["ln2_b"]),
                        "mix_k": col(w["cm_mix_k"]), "mix_r": col(w["cm_mix_r"]), "hmask": hm,
                        "x_orig": np.ascontiguousarray(xT[b, :, t0:t0 + TC])})
    r3 = ph3.run(in_maps)
    return unshard([np.asarray(r["x_out"]) for r in r3], B, D, T)


def kernel(**inp):
    w = _prep_weights(inp)
    x = np.ascontiguousarray(np.asarray(inp["L0_attn_a_x"], np.float32).transpose(0, 2, 1))
    eye = np.eye(128).astype(ml_dtypes.bfloat16)

    x = run_attn_a(x, w)
    rw = Phase("wprep", build_wprep).run([
        {"w_gate": w["w_gate"], "w_up": w["w_up"], "w_down": w["w_down"], "eye": eye}])[0]
    wq = {k: np.asarray(rw[k]) for k in ("wgT", "wuT", "wdT")}
    x = run_ffn(x, wq, eye)
    x = run_attn_b(x, w)
    x = run_ffn(x, wq, eye)
    return np.ascontiguousarray(x.transpose(0, 2, 1))
